# version 32
# baseline (speedup 1.0000x reference)
import contextlib
import numpy as np
import concourse.bass as bass
import concourse.mybir as mybir
from concourse.bass_utils import run_bass_kernel_spmd

F32 = mybir.dt.float32
BF16 = mybir.dt.bfloat16
I32 = mybir.dt.int32
AF = mybir.ActivationFunctionType
ALU = mybir.AluOpType
AX = mybir.AxisListType

CORES = 8
D = 2048
T = 2048
HD = 128
O = 5504
KC = D // 128
OC = 43
BB = 6
NEG = -1.0e30
EPS = 1e-5
INVSQ = 1.0 / float(np.sqrt(128.0))
DEBUG = False


class SemC:
    def __init__(self, nc, es, name):
        self.h = es.enter_context(nc.semaphore(name))
        self.n = 0


def build():
    nc = bass.Bass("TRN2", target_bir_lowering=False, debug=False)

    x_d = nc.dram_tensor("x", [D, T], F32, kind="ExternalInput").ap()
    wq_d = nc.dram_tensor("wq", [D, HD], F32, kind="ExternalInput").ap()
    wk_d = nc.dram_tensor("wk", [D, HD], F32, kind="ExternalInput").ap()
    wv_d = nc.dram_tensor("wv", [D, HD], F32, kind="ExternalInput").ap()
    wo_d = nc.dram_tensor("wo", [HD, D], F32, kind="ExternalInput").ap()
    bq_d = nc.dram_tensor("bq", [HD, 1], F32, kind="ExternalInput").ap()
    bk_d = nc.dram_tensor("bk", [HD, 1], F32, kind="ExternalInput").ap()
    bv_d = nc.dram_tensor("bv", [HD, 1], F32, kind="ExternalInput").ap()
    bo_d = nc.dram_tensor("bo", [D, 1], F32, kind="ExternalInput").ap()
    lut_d = nc.dram_tensor("lut", [16], F32, kind="ExternalInput").ap()
    gidx_d = nc.dram_tensor("gidx", [D, BB * 128], I32, kind="ExternalInput").ap()
    uidx_d = nc.dram_tensor("uidx", [D, BB * 128], I32, kind="ExternalInput").ap()
    didx_d = nc.dram_tensor("didx", [BB * 128, D], I32, kind="ExternalInput").ap()
    y_d = nc.dram_tensor("y", [D, T], F32, kind="ExternalOutput").ap()
    if DEBUG:
        dbg_q = nc.dram_tensor("dbg_q", [128, T], BF16, kind="ExternalOutput").ap()
        dbg_k = nc.dram_tensor("dbg_k", [128, T], BF16, kind="ExternalOutput").ap()
        dbg_v = nc.dram_tensor("dbg_v", [128, T], BF16, kind="ExternalOutput").ap()
        dbg_vtm = nc.dram_tensor("dbg_vtm", [128, T], BF16, kind="ExternalOutput").ap()
        dbg_probs1 = nc.dram_tensor("dbg_probs1", [128, 256], BF16, kind="ExternalOutput").ap()
        dbg_probsT1 = nc.dram_tensor("dbg_probsT1", [128, 256], BF16, kind="ExternalOutput").ap()
        dbg_mx1 = nc.dram_tensor("dbg_mx1", [128, 1], F32, kind="ExternalOutput").ap()
        dbg_rs1 = nc.dram_tensor("dbg_rs1", [128, 1], F32, kind="ExternalOutput").ap()
        dbg_attn = nc.dram_tensor("dbg_attn", [128, T], BF16, kind="ExternalOutput").ap()
        dbg_rstd = nc.dram_tensor("dbg_rstd", [128, T], BF16, kind="ExternalOutput").ap()
        dbg_xn = nc.dram_tensor("dbg_xn", [16, 128, 1024], BF16, kind="ExternalOutput").ap()
        dbg_wg = nc.dram_tensor("dbg_wg", [CORES, BB, 128, KC, 128], BF16, kind="ExternalOutput").ap()
        dbg_wd = nc.dram_tensor("dbg_wd", [CORES, KC, 128, BB, 128], BF16, kind="ExternalOutput").ap()
        dbg_xn2 = nc.dram_tensor("dbg_xn2", [128, KC, 1024], BF16, kind="ExternalOutput").ap()
        dbg_h = nc.dram_tensor("dbg_h", [128, OC, 1024], BF16, kind="ExternalOutput").ap()
        dbg_qp = nc.dram_tensor("dbg_qp", [128, 1024], F32, kind="ExternalOutput").ap()


    x1_d = nc.dram_tensor("x1scr", [D, T], F32).ap()
    mean_scr = nc.dram_tensor("mean_scr", [1, T], F32).ap()
    dumi = nc.dram_tensor("dumi", [128, 16], F32).ap()
    dumo = nc.dram_tensor("dumo", [8, 128, 16], F32, addr_space="Shared").ap()
    agi_g = nc.dram_tensor("agi_g", [BB, 128, KC, 128], BF16).ap()
    agi_u = nc.dram_tensor("agi_u", [BB, 128, KC, 128], BF16).ap()
    agi_dn = nc.dram_tensor("agi_dn", [KC, 128, BB, 128], BF16).ap()
    ago_g = nc.dram_tensor("ago_g", [CORES, BB, 128, KC, 128], BF16,
                           addr_space="Shared").ap()
    ago_u = nc.dram_tensor("ago_u", [CORES, BB, 128, KC, 128], BF16,
                           addr_space="Shared").ap()
    ago_dn = nc.dram_tensor("ago_dn", [CORES, KC, 128, BB, 128], BF16,
                            addr_space="Shared").ap()

    es = contextlib.ExitStack()
    sb = lambda name, shape, dt: es.enter_context(nc.sbuf_tensor(name, shape, dt))

    lutb = sb("lutb", [128, 16], F32)
    iota_p = sb("iota_p", [128, 1], F32)
    ones_c = sb("ones_c", [128, 1], BF16)
    ones_r = sb("ones_r", [1, 128], BF16)
    rstd_b = sb("rstd_b", [128, T], BF16)
    m_b = sb("m_b", [128, T], F32)
    gi = sb("gi", [128, 2048], I32)
    gibf = sb("gibf", [128, 2048], BF16)
    gu = sb("gu", [128, 2048], BF16)
    gw = sb("gw", [128, 2, 2048], BF16)
    xw = sb("xw", [128, 2, 1024], F32)
    y_sb = sb("y_sb", [128, 1, 1024], F32)
    bqkv = sb("bqkv", [128, 3], F32)
    bo_sb = sb("bo_sb", [128, KC], F32)

    base = nc._sbuf_addr_for_side("left")

    class Region:
        def __init__(self, off):
            self.off = off

        def take(self, name, shape, dt):
            nbytes = int(np.prod(shape[1:])) * mybir.dt.size(dt)
            t = nc.alloc_sbuf_tensor_at(name, shape, dt, offset=self.off)
            self.off += (nbytes + 63) // 64 * 64
            assert self.off <= nc.SBUF_PARTITION_SIZE_BYTES - 1024, (name, self.off)
            return t.ap()

    A = Region(base)
    iota_f = A.take("iota_f", [128, 128], F32)
    ident = A.take("ident", [128, 128], BF16)
    cmask = A.take("cmask", [128, 128], F32)
    stt1 = A.take("stt1", [1, T], F32)
    stt2 = A.take("stt2", [1, T], F32)
    stt3 = A.take("stt3", [1, T], F32)
    st_bf = A.take("st_bf", [1, 2 * T], BF16)
    wqkv_f = A.take("wqkv_f", [128, KC, 128], F32)
    wq_bf = A.take("wq_bf", [128, KC, 128], BF16)
    wk_bf = A.take("wk_bf", [128, KC, 128], BF16)
    wv_bf = A.take("wv_bf", [128, KC, 128], BF16)
    wo_bf = A.take("wo_bf", [128, KC, 128], BF16)
    q_bf = A.take("q_bf", [128, T], BF16)
    k_bf = A.take("k_bf", [128, T], BF16)
    v_bf = A.take("v_bf", [128, T], BF16)
    v_tm = A.take("v_tm", [128, KC, 128], BF16)
    probs = A.take("probs", [128, T], BF16)
    probsT = A.take("probsT", [128, T], BF16)
    attn = A.take("attn", [128, T], BF16)
    xnc = A.take("xnc", [128, 2, 1024], BF16)
    xcbf = A.take("xcbf", [128, 4096], BF16)
    sqbf = A.take("sqbf", [128, 4096], BF16)
    x1h = A.take("x1h", [128, 2, 1024], F32)
    xa = A.take("xa", [128, 2, 2048], F32)
    mx = A.take("mx", [128, 1], F32)
    rs = A.take("rs", [128, 1], F32)
    rcp = A.take("rcp", [128, 1], F32)
    rct = A.take("rct", [128, 1], F32)

    B = Region(base)
    h_sb = B.take("h_sb", [128, OC, 1024], BF16)
    xn2 = B.take("xn2", [128, KC, 1024], BF16)
    wgu = [B.take(f"wgu{i}", [128, KC, 128], BF16) for i in range(3)]
    wdn = [B.take(f"wdn{i}", [128, 48, 128], BF16) for i in range(2)]

    psum = es.enter_context(nc.psum_tensor("psum", [128, 4096], F32))
    ps_tr = psum[:, 3072:3584].bitcast(BF16)
    ps_out = psum[:, 3584:3712]

    s_ld = SemC(nc, es, "s_ld")
    s_xa = [SemC(nc, es, f"s_xa{i}") for i in range(2)]
    s_xw = [SemC(nc, es, f"s_xw{i}") for i in range(2)]
    s_dum = SemC(nc, es, "s_dum")
    s_gi2 = [SemC(nc, es, f"s_gi{i}") for i in range(2)]
    s_st2 = [SemC(nc, es, f"s_st{i}") for i in range(2)]
    s_st3 = [SemC(nc, es, f"s_su{i}") for i in range(2)]
    s_wgu = [SemC(nc, es, f"s_wgu{i}") for i in range(3)]
    s_wdn = [SemC(nc, es, f"s_wdn{i}") for i in range(3)]
    s_out = SemC(nc, es, "s_out")
    s_x1w2 = [SemC(nc, es, f"s_x1w{i}") for i in range(2)]
    s_mb = SemC(nc, es, "s_mb")
    s_cc = SemC(nc, es, "s_cc")
    s_v = SemC(nc, es, "s_v")
    s_t = SemC(nc, es, "s_t")
    s_sc = SemC(nc, es, "s_sc")
    s_g = SemC(nc, es, "s_g")
    s_bar = SemC(nc, es, "s_bar")

    ops = {"sync": [], "gpsimd": [], "vector": [], "tensor": [], "scalar": []}

    def op(eng, fn):
        ops[eng].append(fn)

    def wait(eng, s, val):
        if val > 0:
            ops[eng].append(lambda e, h=s.h, v=val: e.wait_ge(h, v))

    def dma(eng, out, in_, s):
        s.n += 16
        ops[eng].append(lambda e, o=out, i=in_, h=s.h: e.dma_start(out=o, in_=i).then_inc(h, 16))
        return s.n

    def cop(eng, fn, s=None):
        if s is None:
            ops[eng].append(lambda e, f=fn: f(e))
            return None
        s.n += 1
        ops[eng].append(lambda e, f=fn, h=s.h: f(e).then_inc(h, 1))
        return s.n

    xw_free = {0: None, 1: None}
    x1h_free = {0: None, 1: None}
    ysb_free = {0: None, 1: None}
    psB_free = {0: None, 1: None, 2: None}
    wgu_free = {0: None, 1: None, 2: None}
    wdn_free = {0: None, 1: None}

    def wait_free(eng, entry):
        if entry is not None:
            wait(eng, entry[0], entry[1])

    dma("sync", lutb[:], lut_d.partition_broadcast(128), s_ld)
    lut_loaded = s_ld.n
    dma("sync", bqkv[:, 0:1], bq_d, s_ld)
    dma("sync", bqkv[:, 1:2], bk_d, s_ld)
    dma("sync", bqkv[:, 2:3], bv_d, s_ld)
    for mm in range(KC):
        dma("sync", bo_sb[:, mm:mm + 1], bo_d[mm * 128:(mm + 1) * 128, :], s_ld)
    small_ld = s_ld.n
    cop("gpsimd", lambda e: e.dma_start(out=dumi, in_=lut_d.partition_broadcast(128)).then_inc(s_dum.h, 16))
    s_dum.n += 16
    wait("gpsimd", s_dum, s_dum.n)
    cop("gpsimd", lambda e: e.collective_compute(
        "AllGather", ALU.bypass, replica_groups=[list(range(CORES))],
        ins=[dumi.opt()], outs=[dumo.opt()]), s_cc)
    cop("gpsimd", lambda e: e.iota(iota_p[:], pattern=[[0, 1]], channel_multiplier=1,
                                   allow_small_or_imprecise_dtypes=True))
    cop("gpsimd", lambda e: e.iota(iota_f[:], pattern=[[1, 128]], channel_multiplier=0,
                                   allow_small_or_imprecise_dtypes=True), s_g)
    wait("vector", s_g, s_g.n)
    cop("vector", lambda e: e.tensor_scalar(out=ident[:], in0=iota_f[:],
                                            scalar1=iota_p[:, 0:1], scalar2=None,
                                            op0=ALU.is_equal))
    cop("vector", lambda e: e.tensor_scalar(out=cmask[:], in0=iota_f[:],
                                            scalar1=iota_p[:, 0:1], scalar2=NEG,
                                            op0=ALU.is_gt, op1=ALU.mult))
    cop("vector", lambda e: e.memset(ones_c[:], 1.0))
    cop("vector", lambda e: e.memset(ones_r[:], 1.0), s_v)

    wcast = []
    for w_d, w_bf, along_rows in ((wq_d, wq_bf, True), (wk_d, wk_bf, True),
                                  (wv_d, wv_bf, True), (wo_d, wo_bf, False)):
        if wcast:
            wait("sync", s_v, wcast[-1])
        for kk in range(KC):
            src = (w_d[kk * 128:(kk + 1) * 128, :] if along_rows
                   else w_d[:, kk * 128:(kk + 1) * 128])
            v = dma("sync", wqkv_f[:, kk, :], src, s_ld)
        wait("vector", s_ld, v)
        wcast.append(cop("vector", lambda e, b=w_bf: e.tensor_copy(
            out=b[:, :, :], in_=wqkv_f[:, :, :]), s_v))
    wqkv_cast, wo_cast = wcast[2], wcast[3]

    gather_tiles = ([("g", kk) for kk in range(0, KC, 2)]
                    + [("u", kk) for kk in range(0, KC, 2)]
                    + [("d", b) for b in range(BB)])
    g_seq = {t: i for i, t in enumerate(gather_tiles)}
    stage_done_v = {}
    gath_done_v = {}
    cast_done_v = {}
    pump = None

    def gather_idx_load(tile):
        seq = g_seq[tile]
        if seq >= 1:
            wait("gpsimd", s_v, cast_done_v[gather_tiles[seq - 1]])
        if tile[0] in ("g", "u"):
            idx_d = gidx_d if tile[0] == "g" else uidx_d
            kk = tile[1]
            v0 = dma("gpsimd", gi[:, 0:768],
                     idx_d[kk * 128:(kk + 1) * 128, :], s_gi2[0])
            v1 = dma("gpsimd", gi[:, 768:1536],
                     idx_d[(kk + 1) * 128:(kk + 2) * 128, :], s_gi2[1])
            return [(s_gi2[0], v0), (s_gi2[1], v1)]
        _, b = tile
        v0 = dma("gpsimd", gi[:, 0:2048],
                 didx_d[b * 128:(b + 1) * 128, :], s_gi2[0])
        return [(s_gi2[0], v0)]

    def gather_stage(tile):
        slot = g_seq[tile] % 2
        wait("gpsimd", s_v, gath_done_v[tile])
        if tile[0] in ("g", "u"):
            kk = tile[1]
            agi = agi_g if tile[0] == "g" else agi_u
            done = []
            for h in range(2):
                sem = (s_st2 if h == 0 else s_st3)[slot]
                dst = agi[:, :, kk + h, :].transpose([1, 0, 2])
                src = gw[:, slot, h * 768:(h + 1) * 768].rearrange(
                    "p (b j) -> p b j", b=BB)
                done.append((sem, dma("gpsimd", dst, src, sem)))
        else:
            _, b = tile
            done = []
            for hf in range(2):
                sem = (s_st2 if hf == 0 else s_st3)[slot]
                dst = agi_dn[hf * 8:(hf + 1) * 8, :, b, :].transpose([1, 0, 2])
                src = gw[:, slot, hf * 1024:(hf + 1) * 1024].rearrange(
                    "p (m j) -> p m j", m=8)
                done.append((sem, dma("gpsimd", dst, src, sem)))
        stage_done_v[tile] = done

    class GPump:
        def __init__(self, tiles):
            self.tiles = list(tiles)
            self.ti = 0
            self.q = []
            self.cur = None

        def _refill(self):
            if self.ti >= len(self.tiles):
                return False
            tile = self.tiles[self.ti]
            self.ti += 1
            self.cur = tile
            seq = g_seq[tile]
            slot = seq % 2
            F = 2048 if tile[0] == "d" else 1536
            iv = gather_idx_load(tile)
            ops_ = []
            def first(e=None, f=F, t=tile, iv=iv, seq=seq):
                for sem0, v0 in iv:
                    wait("vector", sem0, v0)
                if seq >= 2:
                    for sem0, v0 in stage_done_v[gather_tiles[seq - 2]]:
                        wait("vector", sem0, v0)
                v0 = cop("vector", lambda e2, f=f: e2.tensor_copy(
                    out=gibf[:, 0:f], in_=gi[:, 0:f]), s_v)
                cast_done_v[t] = v0
                wait("vector", s_v, v0)
            ops_.append(first)

            def sv_op(fn):
                wait("vector", s_v, s_v.n)
                cop("vector", fn, s_v)

            def z0(sl=slot, f=F):
                sv_op(lambda e2: e2.tensor_scalar(
                    out=gw[:, sl, 0:f], in0=gibf[:, 0:f], scalar1=0.0,
                    scalar2=lutb[:, 0:1], op0=ALU.is_equal, op1=ALU.mult))
            ops_.append(z0)
            for j in range(1, 16):
                def uo(jj=j, f=F):
                    sv_op(lambda e2: e2.tensor_scalar(
                        out=gu[:, 0:f], in0=gibf[:, 0:f], scalar1=float(jj),
                        scalar2=lutb[:, jj:jj + 1], op0=ALU.is_equal, op1=ALU.mult))
                ops_.append(uo)
                if j < 15:
                    def ao(sl=slot, f=F):
                        sv_op(lambda e2: e2.tensor_tensor(
                            out=gw[:, sl, 0:f], in0=gw[:, sl, 0:f], in1=gu[:, 0:f],
                            op=ALU.add))
                    ops_.append(ao)
                else:
                    def alast(sl=slot, f=F, t=tile):
                        wait("vector", s_v, s_v.n)
                        cop("vector", lambda e2: e2.tensor_tensor(
                            out=gw[:, sl, 0:f], in0=gw[:, sl, 0:f], in1=gu[:, 0:f],
                            op=ALU.add), s_v)
                        gath_done_v[t] = s_v.n
                        gather_stage(t)
                    ops_.append(alast)
            self.q = ops_
            return True

        def pump(self, n):
            k = 0
            while k < n:
                if not self.q:
                    if not self._refill():
                        return
                self.q.pop(0)()
                k += 1

        def flush_matrix(self, mat):
            while any(t[0] == mat for t in self.tiles[self.ti:]) or (
                    self.cur is not None and self.cur[0] == mat and self.q):
                if not self.q and not self._refill():
                    return
                while self.q:
                    self.q.pop(0)()

        def done_through(self, idx):
            return self.ti > idx + 1 or (self.ti == idx + 1 and not self.q)

    def serial_v(fn):
        v0 = cop("vector", fn, s_v)
        wait("vector", s_v, v0)
        return v0

    def ln_stats_math_and_bcast():
        wait("vector", s_v, s_v.n)
        mn = serial_v(lambda e: e.tensor_scalar(
            out=stt1[:], in0=stt1[:], scalar1=1.0 / D, scalar2=None, op0=ALU.mult))
        wait("scalar", s_v, mn)
        serial_v(lambda e: e.tensor_scalar(
            out=stt2[:], in0=stt2[:], scalar1=1.0 / D, scalar2=None, op0=ALU.mult))
        serial_v(lambda e: e.tensor_tensor(
            out=stt3[:], in0=stt1[:], in1=stt1[:], op=ALU.mult))
        serial_v(lambda e: e.tensor_tensor(
            out=stt2[:], in0=stt2[:], in1=stt3[:], op=ALU.subtract))
        v = serial_v(lambda e: e.tensor_scalar(
            out=stt2[:], in0=stt2[:], scalar1=EPS, scalar2=None, op0=ALU.add))
        wait("scalar", s_v, v)
        v = cop("scalar", lambda e: e.activation(out=stt3[:], in_=stt2[:],
                                                 func=AF.Ln), s_sc)
        wait("scalar", s_sc, v)
        v = cop("scalar", lambda e: e.activation(out=st_bf[:, 0:T], in_=stt3[:],
                                                 func=AF.Exp, scale=-0.5), s_sc)
        wait("vector", s_sc, v)
        mv = dma("scalar", mean_scr, stt1[0:1, :], s_mb)
        wait("scalar", s_mb, mv)
        mv = dma("scalar", m_b[:], mean_scr.partition_broadcast(128), s_mb)
        wait("tensor", s_sc, v)
        tv = None
        for p in range(4):
            tv = cop("tensor", lambda e, pp=p: e.matmul(
                psum[:, pp * 512:(pp + 1) * 512], ones_r[:],
                st_bf[:, pp * 512:(pp + 1) * 512],
                start=True, stop=True), s_t)
        wait("vector", s_t, tv)
        out_v = cop("vector", lambda e: e.tensor_copy(
            out=rstd_b[:], in_=psum[:, 0:T]), s_v)
        wait("tensor", s_v, out_v)
        return out_v, mv

    pump = GPump(gather_tiles)

    xa_free = {0: None, 1: None}
    xc_free = {0: None, 1: None}
    cast_done = {}
    sq_done = {}
    stat_done = {}
    for kk in range(KC):
        sl = kk % 2
        wait_free("sync", xa_free[sl])
        xv = dma("sync", xa[:, sl, :], x_d[kk * 128:(kk + 1) * 128, :], s_xa[sl])
        wait("vector", s_xa[sl], xv)
        if kk >= 2:
            wait("vector", s_t, stat_done[kk - 2])
        cast_done[kk] = cop("vector", lambda e, q0=sl: e.tensor_copy(
            out=xcbf[:, q0 * 2048:(q0 + 1) * 2048], in_=xa[:, q0, :]), s_v)
        xa_free[sl] = (s_v, cast_done[kk])
        wait("scalar", s_v, cast_done[kk])
        sq_done[kk] = cop("scalar", lambda e, q0=sl: e.activation(
            out=sqbf[:, q0 * 2048:(q0 + 1) * 2048],
            in_=xcbf[:, q0 * 2048:(q0 + 1) * 2048], func=AF.Square), s_sc)
        wait("tensor", s_v, cast_done[kk])
        wait("tensor", s_sc, sq_done[kk])
        for p in range(4):
            cop("tensor", lambda e, q0=sl, pp=p, k0=kk: e.matmul(
                psum[0:1, pp * 512:(pp + 1) * 512], ones_c[:],
                xcbf[:, q0 * 2048 + pp * 512:q0 * 2048 + (pp + 1) * 512],
                start=(k0 == 0), stop=(k0 == KC - 1)))
        for p in range(4):
            stat_done[kk] = cop("tensor", lambda e, q0=sl, pp=p, k0=kk: e.matmul(
                psum[0:1, 2048 + pp * 512:2048 + (pp + 1) * 512], ones_c[:],
                sqbf[:, q0 * 2048 + pp * 512:q0 * 2048 + (pp + 1) * 512],
                start=(k0 == 0), stop=(k0 == KC - 1)), s_t)
        pump.pump(20)
    sqst_t = stat_done[KC - 1]
    wait("vector", s_t, sqst_t)
    v = cop("vector", lambda e: e.tensor_copy(out=stt1[:], in_=psum[0:1, 0:2048]), s_v)
    wait("vector", s_v, v)
    v = cop("vector", lambda e: e.tensor_copy(out=stt2[:], in_=psum[0:1, 2048:4096]), s_v)
    wait("tensor", s_v, v)
    ln1_done, ln1_mb = ln_stats_math_and_bcast()
    ag_done = {"g": None, "u": None, "d": None}

    def trigger_ag(key, agi, ago):
        wait("gpsimd", s_st2[0], s_st2[0].n)
        wait("gpsimd", s_st2[1], s_st2[1].n)
        wait("gpsimd", s_st3[0], s_st3[0].n)
        wait("gpsimd", s_st3[1], s_st3[1].n)
        cop("gpsimd", lambda e, i0=agi, o0=ago: e.collective_compute(
            "AllGather", ALU.bypass, replica_groups=[list(range(CORES))],
            ins=[i0.opt()], outs=[o0.opt()]), s_cc)
        ag_done[key] = s_cc.n

    if DEBUG:
        wait("gpsimd", s_v, ln1_done)
        dma("gpsimd", dbg_rstd, rstd_b[:], s_out)
        pass

    xnc_free = {0: None, 1: None}
    qkv_ev = 0
    for th in range(2):
        for kk in range(KC):
            slot = kk % 2
            wait_free("sync", xw_free[slot])
            xv = dma("sync", xw[:, slot, :],
                     x_d[kk * 128:(kk + 1) * 128, th * 1024:(th + 1) * 1024], s_xw[slot])
            wait("vector", s_xw[slot], xv)
            if th == 0 and kk == 0:
                wait("vector", s_mb, ln1_mb)
            wait_free("vector", xnc_free[slot])
            v0 = cop("vector", lambda e, t0=th, sl=slot: e.tensor_tensor(
                out=xw[:, sl, :], in0=xw[:, sl, :],
                in1=m_b[:, t0 * 1024:(t0 + 1) * 1024], op=ALU.subtract), s_v)
            wait("vector", s_v, v0)
            xn_v = cop("vector", lambda e, t0=th, sl=slot: e.tensor_tensor(
                out=xnc[:, sl, :], in0=xw[:, sl, :],
                in1=rstd_b[:, t0 * 1024:(t0 + 1) * 1024], op=ALU.mult), s_v)
            xw_free[slot] = (s_v, xn_v)
            if DEBUG and th == 0:
                wait("gpsimd", s_v, xn_v)
                dma("gpsimd", dbg_xn[kk], xnc[:, kk % 2, :], s_out)
                _dv = s_out.n
                wait("vector", s_out, _dv)
            wait("tensor", s_v, xn_v)
            if th == 0 and kk == 0:
                wait("tensor", s_v, wqkv_cast)
                wait("tensor", s_v, ln1_done)
            if th == 1 and kk == 0:
                wait("tensor", s_v, qkv_ev)
            qk_t = None
            for wi, wbf in enumerate((wq_bf, wk_bf, wv_bf)):
                for p in range(2):
                    qk_t = cop("tensor", lambda e, w=wbf, k0=kk, wi0=wi, pp=p, sl=slot: e.matmul(
                        psum[:, wi0 * 1024 + pp * 512: wi0 * 1024 + (pp + 1) * 512],
                        w[:, k0, :],
                        xnc[:, sl, pp * 512:(pp + 1) * 512],
                        start=(k0 == 0), stop=(k0 == KC - 1)),
                        s_t if (wi == 2 and p == 1) else None)
            xnc_free[slot] = (s_t, qk_t)
            pump.pump(5)
        wait("vector", s_t, qk_t)
        if DEBUG and th == 0:
            cop("vector", lambda e: e.tensor_copy(out=x1h[:, 0, :], in_=psum[:, 0:1024]), s_v)
            wait("gpsimd", s_v, s_v.n)
            dma("gpsimd", dbg_qp, x1h[:, 0, :], s_out)
            wait("vector", s_out, s_out.n)
        cop("vector", lambda e, t0=th: e.tensor_scalar(
            out=q_bf[:, t0 * 1024:(t0 + 1) * 1024], in0=psum[:, 0:1024],
            scalar1=bqkv[:, 0:1], scalar2=INVSQ, op0=ALU.add, op1=ALU.mult))
        cop("vector", lambda e, t0=th: e.tensor_scalar(
            out=k_bf[:, t0 * 1024:(t0 + 1) * 1024], in0=psum[:, 1024:2048],
            scalar1=bqkv[:, 1:2], scalar2=None, op0=ALU.add))
        qkv_ev = cop("vector", lambda e, t0=th: e.tensor_scalar(
            out=v_bf[:, t0 * 1024:(t0 + 1) * 1024], in0=psum[:, 2048:3072],
            scalar1=bqkv[:, 2:3], scalar2=None, op0=ALU.add), s_v)

    wait("tensor", s_v, qkv_ev)
    ev = 0
    for r0 in range(0, KC, 8):
        tv = None
        for j in range(r0, r0 + 8):
            tv = cop("tensor", lambda e, jj=j, sl=j - r0: e.transpose(
                ps_tr[:, sl * 128:(sl + 1) * 128],
                v_bf[:, jj * 128:(jj + 1) * 128], ident[:]), s_t)
        wait("vector", s_t, tv)
        ev = cop("vector", lambda e, a=r0: e.tensor_copy(
            out=v_tm[:, a:a + 8, :].rearrange("p a b -> p (a b)"),
            in_=ps_tr[:, 0:1024]), s_v)
        if r0 == 0:
            wait("tensor", s_v, ev)

    if DEBUG:
        wait("gpsimd", s_v, qkv_ev)
        dma("gpsimd", dbg_q, q_bf[:], s_out)
        dma("gpsimd", dbg_k, k_bf[:], s_out)
        dma("gpsimd", dbg_v, v_bf[:], s_out)
        wait("gpsimd", s_v, ev)
        dma("gpsimd", dbg_vtm, v_tm.rearrange("p a b -> p (a b)"), s_out)

    prev_tile_v = ev
    for qt in range(KC):
        valid = (qt + 1) * 128
        wait("tensor", s_v, prev_tile_v)
        npc = (valid + 511) // 512
        sc_t = None
        for p in range(npc):
            lo = p * 512
            hi = min(valid, lo + 512)
            sc_t = cop("tensor", lambda e, q0=qt, l0=lo, h0=hi: e.matmul(
                psum[:, l0:h0], q_bf[:, q0 * 128:(q0 + 1) * 128], k_bf[:, l0:h0],
                start=True, stop=True), s_t)
        wait("vector", s_t, sc_t)
        v0 = cop("vector", lambda e, q0=qt: e.tensor_tensor(
            out=psum[:, q0 * 128:(q0 + 1) * 128], in0=psum[:, q0 * 128:(q0 + 1) * 128],
            in1=cmask[:], op=ALU.add), s_v)
        wait("vector", s_v, v0)
        mx_v = cop("vector", lambda e, v0=valid: e.reduce_max(
            mx[:], psum[:, 0:v0], axis=AX.X, negate=True), s_v)
        wait("scalar", s_v, mx_v)
        ex_v = cop("scalar", lambda e, v0=valid: e.activation(
            out=probs[:, 0:v0], in_=psum[:, 0:v0], func=AF.Exp,
            bias=mx[:, 0:1], scale=1.0), s_sc)
        wait("vector", s_sc, ex_v)
        rs_v = cop("vector", lambda e, v0=valid: e.reduce_sum(
            rs[:], probs[:, 0:v0], axis=AX.X), s_v)
        wait("scalar", s_v, rs_v)
        lnv = cop("scalar", lambda e: e.activation(out=rct[:], in_=rs[:],
                                                   func=AF.Ln), s_sc)
        wait("scalar", s_sc, lnv)
        rc_v = cop("scalar", lambda e: e.activation(out=rcp[:], in_=rct[:],
                                                    func=AF.Exp, scale=-1.0), s_sc)
        wait("vector", s_sc, rc_v)
        nrm_v = cop("vector", lambda e, v0=valid: e.tensor_scalar(
            out=probs[:, 0:v0], in0=probs[:, 0:v0], scalar1=rcp[:, 0:1],
            scalar2=None, op0=ALU.mult), s_v)
        if DEBUG and qt == 1:
            wait("gpsimd", s_v, nrm_v)
            dma("gpsimd", dbg_probs1, probs[:, 0:256], s_out)
            dma("gpsimd", dbg_mx1, mx[:], s_out)
            dma("gpsimd", dbg_rs1, rs[:], s_out)
            dbg_t1 = s_out.n
            wait("vector", s_out, dbg_t1)
        wait("tensor", s_v, nrm_v)
        av_t = None
        for r0 in range(0, qt + 1, 8):
            r1 = min(qt + 1, r0 + 8)
            tr_t = None
            for j in range(r0, r1):
                tr_t = cop("tensor", lambda e, jj=j, sl=j - r0: e.transpose(
                    ps_tr[:, sl * 128:(sl + 1) * 128],
                    probs[:, jj * 128:(jj + 1) * 128], ident[:]), s_t)
            wait("vector", s_t, tr_t)
            pev = cop("vector", lambda e, a=r0, b2=r1: e.tensor_copy(
                out=probsT[:, a * 128:b2 * 128], in_=ps_tr[:, 0:(b2 - a) * 128]), s_v)
            wait("tensor", s_v, pev)
            for j in range(r0, r1):
                av_t = cop("tensor", lambda e, jj=j, q0=qt: e.matmul(
                    ps_out[:, :], v_tm[:, jj, :], probsT[:, jj * 128:(jj + 1) * 128],
                    start=(jj == 0), stop=(jj == q0)),
                    s_t if j == qt else None)
        wait("vector", s_t, av_t)
        prev_tile_v = cop("vector", lambda e, q0=qt: e.tensor_copy(
            out=attn[:, q0 * 128:(q0 + 1) * 128], in_=ps_out[:, :]), s_v)
        pump.pump(10)
        if ag_done["g"] is None and pump.done_through(KC // 2 - 1):
            trigger_ag("g", agi_g, ago_g)
        if DEBUG and qt == 1:
            wait("gpsimd", s_v, prev_tile_v)
            dma("gpsimd", dbg_probsT1, probsT[:, 0:256], s_out)
            dbg_t2 = s_out.n
            wait("vector", s_out, dbg_t2)
    attn_v = prev_tile_v
    if DEBUG:
        wait("gpsimd", s_v, attn_v)
        dma("gpsimd", dbg_attn, attn[:], s_out)

    x1wv = {}
    prev_stt = attn_v
    st2_t = 0
    for th in range(2):
        for mm in range(KC):
            slot = mm % 2
            wait_free("sync", xw_free[slot])
            xv = dma("sync", xw[:, slot, :],
                     x_d[mm * 128:(mm + 1) * 128, th * 1024:(th + 1) * 1024], s_xw[slot])
            wait("tensor", s_v, prev_stt)
            if th == 0 and mm == 0:
                wait("tensor", s_v, wo_cast)
            pp_t = None
            for p in range(2):
                pp_t = cop("tensor", lambda e, m0=mm, t0=th, pp=p: e.matmul(
                    psum[:, pp * 512:(pp + 1) * 512], wo_bf[:, m0, :],
                    attn[:, t0 * 1024 + pp * 512:t0 * 1024 + (pp + 1) * 512],
                    start=True, stop=True), s_t if p == 1 else None)
            wait("vector", s_t, pp_t)
            wait("vector", s_xw[slot], xv)
            wait_free("vector", x1h_free[slot])
            if th > 0 or mm > 0:
                wait("vector", s_sc, sq_v)
                wait("vector", s_t, st2_t)
            v0 = cop("vector", lambda e, m0=mm, sl=slot: e.scalar_tensor_tensor(
                out=x1h[:, sl, :], in0=psum[:, 0:1024], scalar=bo_sb[:, m0:m0 + 1],
                in1=xw[:, sl, :], op0=ALU.add, op1=ALU.add), s_v)
            wait("vector", s_v, v0)
            x1_v = cop("vector", lambda e, sl=slot: e.tensor_copy(
                out=xcbf[:, 0:1024], in_=x1h[:, sl, :]), s_v)
            prev_stt = x1_v
            xw_free[slot] = (s_v, x1_v)
            wait("sync", s_v, x1_v)
            x1wv[(th, mm)] = dma(
                "sync", x1_d[mm * 128:(mm + 1) * 128, th * 1024:(th + 1) * 1024],
                x1h[:, slot, :], s_x1w2[slot])
            x1h_free[slot] = (s_x1w2[slot], x1wv[(th, mm)])
            wait("scalar", s_v, x1_v)
            sq_v = cop("scalar", lambda e: e.activation(
                out=sqbf[:, 0:1024], in_=xcbf[:, 0:1024], func=AF.Square), s_sc)
            wait("tensor", s_v, x1_v)
            wait("tensor", s_sc, sq_v)
            if th == 1 and mm == 0:
                wait("tensor", s_v, st_ev)
            for p in range(2):
                cop("tensor", lambda e, m0=mm, pp=p: e.matmul(
                    psum[0:1, 2048 + pp * 512:2048 + (pp + 1) * 512], ones_c[:],
                    xcbf[:, pp * 512:(pp + 1) * 512],
                    start=(m0 == 0), stop=(m0 == KC - 1)))
            for p in range(2):
                st2_t = cop("tensor", lambda e, m0=mm, pp=p: e.matmul(
                    psum[0:1, 3072 + pp * 512:3072 + (pp + 1) * 512], ones_c[:],
                    sqbf[:, pp * 512:(pp + 1) * 512],
                    start=(m0 == 0), stop=(m0 == KC - 1)), s_t)
            pump.pump(6)
            if ag_done["g"] is None and pump.done_through(KC // 2 - 1):
                trigger_ag("g", agi_g, ago_g)
            if ag_done["u"] is None and pump.done_through(KC - 1):
                trigger_ag("u", agi_u, ago_u)
            if ag_done["d"] is None and pump.done_through(KC + BB - 1):
                trigger_ag("d", agi_dn, ago_dn)
        wait("vector", s_t, st2_t)
        v0 = cop("vector", lambda e, t0=th: e.tensor_copy(
            out=stt1[0:1, t0 * 1024:(t0 + 1) * 1024], in_=psum[0:1, 2048:3072]), s_v)
        wait("vector", s_v, v0)
        st_ev = cop("vector", lambda e, t0=th: e.tensor_copy(
            out=stt2[0:1, t0 * 1024:(t0 + 1) * 1024], in_=psum[0:1, 3072:4096]), s_v)
    wait("scalar", s_v, qkv_ev)
    ln2_done, ln2_mb = ln_stats_math_and_bcast()

    pump.flush_matrix("g")
    if ag_done["g"] is None:
        trigger_ag("g", agi_g, ago_g)

    if DEBUG:
        dma("gpsimd", dbg_wg, ago_g, s_out)

    x1_all_written = [(s_x1w2[0], s_x1w2[0].n), (s_x1w2[1], s_x1w2[1].n)]
    ag_u_done = None
    ag_d_done = None
    xn2_done = {}
    xr_free = {0: None}

    def emit_xn2_chunk(tt, kk, buf_ap, slot_key, free_map, q="scalar"):
        for sem0, v0 in x1_all_written:
            wait(q, sem0, v0)
        if free_map is not None:
            wait_free(q, free_map[slot_key])
        sem = s_xw[slot_key]
        xv = dma(q, buf_ap,
                 x1_d[kk * 128:(kk + 1) * 128, tt * 1024:(tt + 1) * 1024], sem)
        wait("vector", sem, xv)
        if kk == 0:
            wait("vector", s_mb, ln2_mb)
        v0 = cop("vector", lambda e, b=buf_ap, t0=tt: e.tensor_tensor(
            out=b, in0=b, in1=m_b[:, t0 * 1024:(t0 + 1) * 1024], op=ALU.subtract), s_v)
        wait("vector", s_v, v0)
        v = cop("vector", lambda e, b=buf_ap, k0=kk, t0=tt: e.tensor_tensor(
            out=xn2[:, k0, :], in0=b,
            in1=rstd_b[:, t0 * 1024:(t0 + 1) * 1024], op=ALU.mult), s_v)
        if free_map is not None:
            free_map[slot_key] = (s_v, v)
        return v

    xr_free = {0: None}
    for tt in range(2):
        for kk in range(KC):
            xn2_done[tt] = emit_xn2_chunk(tt, kk, xw[:, kk % 2, :], kk % 2, xw_free)

        if DEBUG and tt == 0:
            wait("gpsimd", s_v, xn2_done[0])
            dma("gpsimd", dbg_xn2, xn2[:, :, :], s_out)

        wait("sync", s_cc, ag_done["g"])
        silu_done = {}
        for g in range(OC):
            slot = g % 3
            r, b = g % 8, g // 8
            wait_free("sync", wgu_free[slot])
            wl = dma("sync", wgu[slot][:, :, :], ago_g[r, b], s_wgu[slot])
            wait("tensor", s_wgu[slot], wl)
            if g == 0:
                wait("tensor", s_v, xn2_done[tt])
            wait_free("tensor", psB_free[slot])
            g_t = None
            for kk in range(KC):
                for p in range(2):
                    g_t = cop("tensor", lambda e, sl=slot, k0=kk, pp=p: e.matmul(
                        psum[:, sl * 1024 + pp * 512:sl * 1024 + (pp + 1) * 512],
                        wgu[sl][:, k0, :], xn2[:, k0, pp * 512:(pp + 1) * 512],
                        start=(k0 == 0), stop=(k0 == KC - 1)),
                        s_t if (kk == KC - 1 and p == 1) else None)
            wgu_free[slot] = (s_t, g_t)
            wait("scalar", s_t, g_t)
            sv = cop("scalar", lambda e, gg=g, sl=slot: e.activation(
                out=h_sb[:, gg, :], in_=psum[:, sl * 1024:sl * 1024 + 1024],
                func=AF.Silu), s_sc)
            psB_free[slot] = (s_sc, sv)
            if tt == 0:
                pump.pump(15)
                if ag_done["u"] is None and pump.done_through(KC - 1):
                    trigger_ag("u", agi_u, ago_u)
                if ag_done["d"] is None and pump.done_through(KC + BB - 1):
                    trigger_ag("d", agi_dn, ago_dn)
        if tt == 0:
            if ag_done["u"] is None:
                pump.flush_matrix("u")
                trigger_ag("u", agi_u, ago_u)
            if ag_done["d"] is None:
                pump.flush_matrix("d")
                trigger_ag("d", agi_dn, ago_dn)

        wait("sync", s_cc, ag_done["u"])
        for g in range(OC):
            slot = g % 3
            r, b = g % 8, g // 8
            wait_free("sync", wgu_free[slot])
            wl = dma("sync", wgu[slot][:, :, :], ago_u[r, b], s_wgu[slot])
            wait("tensor", s_wgu[slot], wl)
            wait_free("tensor", psB_free[slot])
            g_t = None
            for kk in range(KC):
                for p in range(2):
                    g_t = cop("tensor", lambda e, sl=slot, k0=kk, pp=p: e.matmul(
                        psum[:, sl * 1024 + pp * 512:sl * 1024 + (pp + 1) * 512],
                        wgu[sl][:, k0, :], xn2[:, k0, pp * 512:(pp + 1) * 512],
                        start=(k0 == 0), stop=(k0 == KC - 1)),
                        s_t if (kk == KC - 1 and p == 1) else None)
            wgu_free[slot] = (s_t, g_t)
            wait("vector", s_t, g_t)
            hv = cop("vector", lambda e, gg=g, sl=slot: e.tensor_tensor(
                out=h_sb[:, gg, :], in0=h_sb[:, gg, :],
                in1=psum[:, sl * 1024:sl * 1024 + 1024], op=ALU.mult), s_v)
            psB_free[slot] = (s_v, hv)
        h_done_v = hv

        if DEBUG and tt == 0:
            wait("gpsimd", s_v, h_done_v)
            dma("gpsimd", dbg_h, h_sb[:, :, :], s_out)
            dma("gpsimd", dbg_wd, ago_dn, s_out)
            _hv = s_out.n
            wait("scalar", s_out, _hv)

        wait("sync", s_cc, ag_done["d"])
        for mm in range(KC):
            ws3 = mm % 2
            slot = mm % 2
            ps3 = mm % 3
            wait_free("sync", wdn_free[ws3])
            wl = None
            for r in range(8):
                wl = dma("sync", wdn[ws3][:, r * 6:(r + 1) * 6, :], ago_dn[r, mm], s_wdn[ws3])
            wait_free("scalar", xw_free[slot])
            xv = dma("scalar", xw[:, slot, :],
                     x1_d[mm * 128:(mm + 1) * 128, tt * 1024:(tt + 1) * 1024], s_xw[slot])
            wait("tensor", s_wdn[ws3], wl)
            if mm == 0:
                wait("tensor", s_v, h_done_v)
            wait_free("tensor", psB_free[ps3])
            d_t = None
            for g in range(OC):
                wslot = (g % 8) * 6 + g // 8
                for p in range(2):
                    d_t = cop("tensor", lambda e, w3=ws3, sl=ps3, gg=g, ws=wslot, pp=p: e.matmul(
                        psum[:, sl * 1024 + pp * 512:sl * 1024 + (pp + 1) * 512],
                        wdn[w3][:, ws, :], h_sb[:, gg, pp * 512:(pp + 1) * 512],
                        start=(gg == 0), stop=(gg == OC - 1)),
                        s_t if (g == OC - 1 and p == 1) else None)
            wdn_free[ws3] = (s_t, d_t)
            wait("vector", s_t, d_t)
            wait("vector", s_xw[slot], xv)
            wait_free("vector", ysb_free[0])
            yv = cop("vector", lambda e, sl=slot, p3=ps3: e.tensor_tensor(
                out=y_sb[:, 0, :], in0=psum[:, p3 * 1024:p3 * 1024 + 1024],
                in1=xw[:, sl, :], op=ALU.add), s_v)
            psB_free[ps3] = (s_v, yv)
            xw_free[slot] = (s_v, yv)
            wait("gpsimd", s_v, yv)
            ov = dma("gpsimd", y_d[mm * 128:(mm + 1) * 128, tt * 1024:(tt + 1) * 1024],
                     y_sb[:, 0, :], s_out)
            ysb_free[0] = (s_out, ov)



    wait("sync", s_out, s_out.n)
    wait("gpsimd", s_out, s_out.n)

    with nc.Block() as block:
        @block.sync
        def _(e):
            for f in ops["sync"]:
                f(e)

        @block.gpsimd
        def _(e):
            for f in ops["gpsimd"]:
                f(e)

        @block.vector
        def _(e):
            for f in ops["vector"]:
                f(e)

        @block.tensor
        def _(e):
            for f in ops["tensor"]:
                f(e)

        @block.scalar
        def _(e):
            for f in ops["scalar"]:
                f(e)

    es.close()
    return nc


def _prep_inputs(L0_attn_x, Wq, bq, Wk, bk, Wv, bv, Wo, bo, lut, gidx, uidx, didx):
    x = np.asarray(L0_attn_x, np.float32)
    gidx = np.asarray(gidx, np.int32)
    uidx = np.asarray(uidx, np.int32)
    didx = np.asarray(didx, np.int32)
    in_maps = []
    for c in range(CORES):
        gblocks, ublocks, dblocks = [], [], []
        for b in range(BB):
            g = b * 8 + c
            if g >= OC:
                g = OC - 1
            gblocks.append(gidx[g * 128:(g + 1) * 128, :].T)
            ublocks.append(uidx[g * 128:(g + 1) * 128, :].T)
            dblocks.append(didx[:, g * 128:(g + 1) * 128].T)
        in_maps.append({
            "x": np.ascontiguousarray(x[c].T),
            "wq": np.ascontiguousarray(np.asarray(Wq, np.float32)),
            "wk": np.ascontiguousarray(np.asarray(Wk, np.float32)),
            "wv": np.ascontiguousarray(np.asarray(Wv, np.float32)),
            "wo": np.ascontiguousarray(np.asarray(Wo, np.float32)),
            "bq": np.ascontiguousarray(np.asarray(bq, np.float32).reshape(HD, 1)),
            "bk": np.ascontiguousarray(np.asarray(bk, np.float32).reshape(HD, 1)),
            "bv": np.ascontiguousarray(np.asarray(bv, np.float32).reshape(HD, 1)),
            "bo": np.ascontiguousarray(np.asarray(bo, np.float32).reshape(D, 1)),
            "lut": np.ascontiguousarray(np.asarray(lut, np.float32)),
            "gidx": np.ascontiguousarray(np.concatenate(gblocks, axis=1)),
            "uidx": np.ascontiguousarray(np.concatenate(ublocks, axis=1)),
            "didx": np.ascontiguousarray(np.concatenate(dblocks, axis=0)),
        })
    return in_maps


def kernel(L0_attn_x, L0_attn_Wq, L0_attn_bq, L0_attn_Wk, L0_attn_bk,
           L0_attn_Wv, L0_attn_bv, L0_attn_Wo, L0_attn_bo,
           L0_ffn_lut, L0_ffn_gate_idx, L0_ffn_up_idx, L0_ffn_down_idx):
    in_maps = _prep_inputs(L0_attn_x, L0_attn_Wq, L0_attn_bq, L0_attn_Wk,
                           L0_attn_bk, L0_attn_Wv, L0_attn_bv, L0_attn_Wo,
                           L0_attn_bo, L0_ffn_lut, L0_ffn_gate_idx,
                           L0_ffn_up_idx, L0_ffn_down_idx)
    nc = build()
    res = run_bass_kernel_spmd(nc, in_maps, core_ids=list(range(CORES)))
    out = np.stack([np.ascontiguousarray(r["y"].T) for r in res.results])
    return out.astype(np.float32)


# version 33
# speedup vs baseline: 1.1512x; 1.1512x over previous
import contextlib
import numpy as np
import concourse.bass as bass
import concourse.mybir as mybir
from concourse.bass_utils import run_bass_kernel_spmd

F32 = mybir.dt.float32
BF16 = mybir.dt.bfloat16
I32 = mybir.dt.int32
AF = mybir.ActivationFunctionType
ALU = mybir.AluOpType
AX = mybir.AxisListType

CORES = 8
D = 2048
T = 2048
HD = 128
O = 5504
KC = D // 128
OC = 43
BB = 6
NEG = -1.0e30
EPS = 1e-5
INVSQ = 1.0 / float(np.sqrt(128.0))
DEBUG = False


class SemC:
    def __init__(self, nc, es, name):
        self.h = es.enter_context(nc.semaphore(name))
        self.n = 0


def build():
    nc = bass.Bass("TRN2", target_bir_lowering=False, debug=False)

    x_d = nc.dram_tensor("x", [D, T], F32, kind="ExternalInput").ap()
    wq_d = nc.dram_tensor("wq", [D, HD], F32, kind="ExternalInput").ap()
    wk_d = nc.dram_tensor("wk", [D, HD], F32, kind="ExternalInput").ap()
    wv_d = nc.dram_tensor("wv", [D, HD], F32, kind="ExternalInput").ap()
    wo_d = nc.dram_tensor("wo", [HD, D], F32, kind="ExternalInput").ap()
    bq_d = nc.dram_tensor("bq", [HD, 1], F32, kind="ExternalInput").ap()
    bk_d = nc.dram_tensor("bk", [HD, 1], F32, kind="ExternalInput").ap()
    bv_d = nc.dram_tensor("bv", [HD, 1], F32, kind="ExternalInput").ap()
    bo_d = nc.dram_tensor("bo", [D, 1], F32, kind="ExternalInput").ap()
    lut_d = nc.dram_tensor("lut", [16], F32, kind="ExternalInput").ap()
    gidx_d = nc.dram_tensor("gidx", [D, BB * 128], I32, kind="ExternalInput").ap()
    uidx_d = nc.dram_tensor("uidx", [D, BB * 128], I32, kind="ExternalInput").ap()
    didx_d = nc.dram_tensor("didx", [BB * 128, D], I32, kind="ExternalInput").ap()
    y_d = nc.dram_tensor("y", [D, T], F32, kind="ExternalOutput").ap()
    if DEBUG:
        dbg_q = nc.dram_tensor("dbg_q", [128, T], BF16, kind="ExternalOutput").ap()
        dbg_k = nc.dram_tensor("dbg_k", [128, T], BF16, kind="ExternalOutput").ap()
        dbg_v = nc.dram_tensor("dbg_v", [128, T], BF16, kind="ExternalOutput").ap()
        dbg_vtm = nc.dram_tensor("dbg_vtm", [128, T], BF16, kind="ExternalOutput").ap()
        dbg_probs1 = nc.dram_tensor("dbg_probs1", [128, 256], BF16, kind="ExternalOutput").ap()
        dbg_probsT1 = nc.dram_tensor("dbg_probsT1", [128, 256], BF16, kind="ExternalOutput").ap()
        dbg_mx1 = nc.dram_tensor("dbg_mx1", [128, 1], F32, kind="ExternalOutput").ap()
        dbg_rs1 = nc.dram_tensor("dbg_rs1", [128, 1], F32, kind="ExternalOutput").ap()
        dbg_attn = nc.dram_tensor("dbg_attn", [128, T], BF16, kind="ExternalOutput").ap()
        dbg_rstd = nc.dram_tensor("dbg_rstd", [128, T], BF16, kind="ExternalOutput").ap()
        dbg_xn = nc.dram_tensor("dbg_xn", [16, 128, 1024], BF16, kind="ExternalOutput").ap()
        dbg_wg = nc.dram_tensor("dbg_wg", [CORES, BB, 128, KC, 128], BF16, kind="ExternalOutput").ap()
        dbg_wd = nc.dram_tensor("dbg_wd", [CORES, KC, 128, BB, 128], BF16, kind="ExternalOutput").ap()
        dbg_xn2 = nc.dram_tensor("dbg_xn2", [128, KC, 1024], BF16, kind="ExternalOutput").ap()
        dbg_h = nc.dram_tensor("dbg_h", [128, OC, 1024], BF16, kind="ExternalOutput").ap()
        dbg_qp = nc.dram_tensor("dbg_qp", [128, 1024], F32, kind="ExternalOutput").ap()


    x1_d = nc.dram_tensor("x1scr", [D, T], F32).ap()
    mean_scr = nc.dram_tensor("mean_scr", [1, T], F32).ap()
    dumi = nc.dram_tensor("dumi", [128, 16], F32).ap()
    dumo = nc.dram_tensor("dumo", [8, 128, 16], F32, addr_space="Shared").ap()
    agi_g = nc.dram_tensor("agi_g", [BB, 128, KC, 128], BF16).ap()
    agi_u = nc.dram_tensor("agi_u", [BB, 128, KC, 128], BF16).ap()
    agi_dn = nc.dram_tensor("agi_dn", [KC, 128, BB, 128], BF16).ap()
    ago_g = nc.dram_tensor("ago_g", [CORES, BB, 128, KC, 128], BF16,
                           addr_space="Shared").ap()
    ago_u = nc.dram_tensor("ago_u", [CORES, BB, 128, KC, 128], BF16,
                           addr_space="Shared").ap()
    ago_dn = nc.dram_tensor("ago_dn", [CORES, KC, 128, BB, 128], BF16,
                            addr_space="Shared").ap()

    es = contextlib.ExitStack()
    sb = lambda name, shape, dt: es.enter_context(nc.sbuf_tensor(name, shape, dt))

    lutb = sb("lutb", [128, 16], F32)
    iota_p = sb("iota_p", [128, 1], F32)
    ones_c = sb("ones_c", [128, 1], BF16)
    ones_r = sb("ones_r", [1, 128], BF16)
    rstd_b = sb("rstd_b", [128, T], BF16)
    m_b = sb("m_b", [128, T], F32)
    gi = sb("gi", [128, 2048], I32)
    gibf = sb("gibf", [128, 2048], BF16)
    gu = sb("gu", [128, 2048], BF16)
    gw = sb("gw", [128, 2, 2048], BF16)
    xw = sb("xw", [128, 2, 1024], F32)
    y_sb = sb("y_sb", [128, 1, 1024], F32)
    bqkv = sb("bqkv", [128, 3], F32)
    bo_sb = sb("bo_sb", [128, KC], F32)

    base = nc._sbuf_addr_for_side("left")

    class Region:
        def __init__(self, off):
            self.off = off

        def take(self, name, shape, dt):
            nbytes = int(np.prod(shape[1:])) * mybir.dt.size(dt)
            t = nc.alloc_sbuf_tensor_at(name, shape, dt, offset=self.off)
            self.off += (nbytes + 63) // 64 * 64
            assert self.off <= nc.SBUF_PARTITION_SIZE_BYTES - 1024, (name, self.off)
            return t.ap()

    A = Region(base)
    iota_f = A.take("iota_f", [128, 128], F32)
    ident = A.take("ident", [128, 128], BF16)
    cmask = A.take("cmask", [128, 128], F32)
    stt1 = A.take("stt1", [1, T], F32)
    stt2 = A.take("stt2", [1, T], F32)
    stt3 = A.take("stt3", [1, T], F32)
    st_bf = A.take("st_bf", [1, 2 * T], BF16)
    wqkv_f = A.take("wqkv_f", [128, KC, 128], F32)
    wq_bf = A.take("wq_bf", [128, KC, 128], BF16)
    wk_bf = A.take("wk_bf", [128, KC, 128], BF16)
    wv_bf = A.take("wv_bf", [128, KC, 128], BF16)
    wo_bf = A.take("wo_bf", [128, KC, 128], BF16)
    q_bf = A.take("q_bf", [128, T], BF16)
    k_bf = A.take("k_bf", [128, T], BF16)
    v_bf = A.take("v_bf", [128, T], BF16)
    v_tm = A.take("v_tm", [128, KC, 128], BF16)
    probs = A.take("probs", [128, T], BF16)
    probsT = A.take("probsT", [128, T], BF16)
    attn = A.take("attn", [128, T], BF16)
    xnc = A.take("xnc", [128, 2, 1024], BF16)
    xcbf = A.take("xcbf", [128, 4096], BF16)
    sqbf = A.take("sqbf", [128, 4096], BF16)
    x1h = A.take("x1h", [128, 2, 1024], F32)
    xa = A.take("xa", [128, 2, 2048], F32)
    mx = A.take("mx", [128, 1], F32)
    rs = A.take("rs", [128, 1], F32)
    rcp = A.take("rcp", [128, 1], F32)
    rct = A.take("rct", [128, 1], F32)

    B = Region(base)
    h_sb = B.take("h_sb", [128, OC, 1024], BF16)
    xn2 = B.take("xn2", [128, KC, 1024], BF16)
    wgu = [B.take(f"wgu{i}", [128, KC, 128], BF16) for i in range(3)]
    wdn = [B.take(f"wdn{i}", [128, 48, 128], BF16) for i in range(2)]

    psum = es.enter_context(nc.psum_tensor("psum", [128, 4096], F32))
    ps_tr = psum[:, 3072:3584].bitcast(BF16)
    ps_out = psum[:, 3584:3712]

    s_ld = SemC(nc, es, "s_ld")
    s_xa = [SemC(nc, es, f"s_xa{i}") for i in range(2)]
    s_xw = [SemC(nc, es, f"s_xw{i}") for i in range(2)]
    s_dum = SemC(nc, es, "s_dum")
    s_gi2 = [SemC(nc, es, f"s_gi{i}") for i in range(2)]
    s_st2 = [SemC(nc, es, f"s_st{i}") for i in range(2)]
    s_st3 = [SemC(nc, es, f"s_su{i}") for i in range(2)]
    s_wgu = [SemC(nc, es, f"s_wgu{i}") for i in range(3)]
    s_wdn = [SemC(nc, es, f"s_wdn{i}") for i in range(3)]
    s_out = SemC(nc, es, "s_out")
    s_x1w2 = [SemC(nc, es, f"s_x1w{i}") for i in range(2)]
    s_mb = SemC(nc, es, "s_mb")
    s_cc = SemC(nc, es, "s_cc")
    s_v = SemC(nc, es, "s_v")
    s_t = SemC(nc, es, "s_t")
    s_sc = SemC(nc, es, "s_sc")
    s_g = SemC(nc, es, "s_g")
    s_bar = SemC(nc, es, "s_bar")

    ops = {"sync": [], "gpsimd": [], "vector": [], "tensor": [], "scalar": []}

    def op(eng, fn):
        ops[eng].append(fn)

    def wait(eng, s, val):
        if val > 0:
            ops[eng].append(lambda e, h=s.h, v=val: e.wait_ge(h, v))

    def dma(eng, out, in_, s):
        s.n += 16
        ops[eng].append(lambda e, o=out, i=in_, h=s.h: e.dma_start(out=o, in_=i).then_inc(h, 16))
        return s.n

    def cop(eng, fn, s=None):
        if s is None:
            ops[eng].append(lambda e, f=fn: f(e))
            return None
        s.n += 1
        ops[eng].append(lambda e, f=fn, h=s.h: f(e).then_inc(h, 1))
        return s.n

    xw_free = {0: None, 1: None}
    x1h_free = {0: None, 1: None}
    ysb_free = {0: None, 1: None}
    psB_free = {0: None, 1: None, 2: None}
    wgu_free = {0: None, 1: None, 2: None}
    wdn_free = {0: None, 1: None}

    def wait_free(eng, entry):
        if entry is not None:
            wait(eng, entry[0], entry[1])

    dma("sync", lutb[:], lut_d.partition_broadcast(128), s_ld)
    lut_loaded = s_ld.n
    dma("sync", bqkv[:, 0:1], bq_d, s_ld)
    dma("sync", bqkv[:, 1:2], bk_d, s_ld)
    dma("sync", bqkv[:, 2:3], bv_d, s_ld)
    for mm in range(KC):
        dma("sync", bo_sb[:, mm:mm + 1], bo_d[mm * 128:(mm + 1) * 128, :], s_ld)
    small_ld = s_ld.n
    cop("gpsimd", lambda e: e.dma_start(out=dumi, in_=lut_d.partition_broadcast(128)).then_inc(s_dum.h, 16))
    s_dum.n += 16
    wait("gpsimd", s_dum, s_dum.n)
    cop("gpsimd", lambda e: e.collective_compute(
        "AllGather", ALU.bypass, replica_groups=[list(range(CORES))],
        ins=[dumi.opt()], outs=[dumo.opt()]), s_cc)
    cop("gpsimd", lambda e: e.iota(iota_p[:], pattern=[[0, 1]], channel_multiplier=1,
                                   allow_small_or_imprecise_dtypes=True))
    cop("gpsimd", lambda e: e.iota(iota_f[:], pattern=[[1, 128]], channel_multiplier=0,
                                   allow_small_or_imprecise_dtypes=True), s_g)
    wait("vector", s_g, s_g.n)
    cop("vector", lambda e: e.tensor_scalar(out=ident[:], in0=iota_f[:],
                                            scalar1=iota_p[:, 0:1], scalar2=None,
                                            op0=ALU.is_equal))
    cop("vector", lambda e: e.tensor_scalar(out=cmask[:], in0=iota_f[:],
                                            scalar1=iota_p[:, 0:1], scalar2=NEG,
                                            op0=ALU.is_gt, op1=ALU.mult))
    cop("vector", lambda e: e.memset(ones_c[:], 1.0))
    cop("vector", lambda e: e.memset(ones_r[:], 1.0), s_v)

    wcast = []
    for w_d, w_bf, along_rows in ((wq_d, wq_bf, True), (wk_d, wk_bf, True),
                                  (wv_d, wv_bf, True), (wo_d, wo_bf, False)):
        if wcast:
            wait("sync", s_v, wcast[-1])
        for kk in range(KC):
            src = (w_d[kk * 128:(kk + 1) * 128, :] if along_rows
                   else w_d[:, kk * 128:(kk + 1) * 128])
            v = dma("sync", wqkv_f[:, kk, :], src, s_ld)
        wait("vector", s_ld, v)
        wcast.append(cop("vector", lambda e, b=w_bf: e.tensor_copy(
            out=b[:, :, :], in_=wqkv_f[:, :, :]), s_v))
    wqkv_cast, wo_cast = wcast[2], wcast[3]

    gather_tiles = ([("g", kk) for kk in range(0, KC, 2)]
                    + [("u", kk) for kk in range(0, KC, 2)]
                    + [("d", b) for b in range(BB)])
    g_seq = {t: i for i, t in enumerate(gather_tiles)}
    stage_done_v = {}
    gath_done_v = {}
    cast_done_v = {}
    pump = None

    def gather_idx_load(tile):
        seq = g_seq[tile]
        if seq >= 1:
            wait("gpsimd", s_v, cast_done_v[gather_tiles[seq - 1]])
        if tile[0] in ("g", "u"):
            idx_d = gidx_d if tile[0] == "g" else uidx_d
            kk = tile[1]
            v0 = dma("gpsimd", gi[:, 0:768],
                     idx_d[kk * 128:(kk + 1) * 128, :], s_gi2[0])
            v1 = dma("gpsimd", gi[:, 768:1536],
                     idx_d[(kk + 1) * 128:(kk + 2) * 128, :], s_gi2[1])
            return [(s_gi2[0], v0), (s_gi2[1], v1)]
        _, b = tile
        v0 = dma("gpsimd", gi[:, 0:2048],
                 didx_d[b * 128:(b + 1) * 128, :], s_gi2[0])
        return [(s_gi2[0], v0)]

    def gather_stage(tile):
        slot = g_seq[tile] % 2
        wait("gpsimd", s_v, gath_done_v[tile])
        if tile[0] in ("g", "u"):
            kk = tile[1]
            agi = agi_g if tile[0] == "g" else agi_u
            done = []
            for h in range(2):
                sem = (s_st2 if h == 0 else s_st3)[slot]
                dst = agi[:, :, kk + h, :].transpose([1, 0, 2])
                src = gw[:, slot, h * 768:(h + 1) * 768].rearrange(
                    "p (b j) -> p b j", b=BB)
                done.append((sem, dma("gpsimd", dst, src, sem)))
        else:
            _, b = tile
            done = []
            for hf in range(2):
                sem = (s_st2 if hf == 0 else s_st3)[slot]
                dst = agi_dn[hf * 8:(hf + 1) * 8, :, b, :].transpose([1, 0, 2])
                src = gw[:, slot, hf * 1024:(hf + 1) * 1024].rearrange(
                    "p (m j) -> p m j", m=8)
                done.append((sem, dma("gpsimd", dst, src, sem)))
        stage_done_v[tile] = done

    class GPump:
        def __init__(self, tiles):
            self.tiles = list(tiles)
            self.ti = 0
            self.q = []
            self.cur = None

        def _refill(self):
            if self.ti >= len(self.tiles):
                return False
            tile = self.tiles[self.ti]
            self.ti += 1
            self.cur = tile
            seq = g_seq[tile]
            slot = seq % 2
            F = 2048 if tile[0] == "d" else 1536
            iv = gather_idx_load(tile)
            ops_ = []
            def first(e=None, f=F, t=tile, iv=iv, seq=seq):
                for sem0, v0 in iv:
                    wait("vector", sem0, v0)
                if seq >= 2:
                    for sem0, v0 in stage_done_v[gather_tiles[seq - 2]]:
                        wait("vector", sem0, v0)
                v0 = cop("vector", lambda e2, f=f: e2.tensor_copy(
                    out=gibf[:, 0:f], in_=gi[:, 0:f]), s_v)
                cast_done_v[t] = v0
                wait("vector", s_v, v0)
            ops_.append(first)

            def sv_op(fn):
                wait("vector", s_v, s_v.n)
                cop("vector", fn, s_v)

            def z0(sl=slot, f=F):
                sv_op(lambda e2: e2.tensor_scalar(
                    out=gw[:, sl, 0:f], in0=gibf[:, 0:f], scalar1=0.0,
                    scalar2=lutb[:, 0:1], op0=ALU.is_equal, op1=ALU.mult))
            ops_.append(z0)
            for j in range(1, 16):
                def uo(jj=j, f=F):
                    sv_op(lambda e2: e2.tensor_scalar(
                        out=gu[:, 0:f], in0=gibf[:, 0:f], scalar1=float(jj),
                        scalar2=lutb[:, jj:jj + 1], op0=ALU.is_equal, op1=ALU.mult))
                ops_.append(uo)
                if j < 15:
                    def ao(sl=slot, f=F):
                        sv_op(lambda e2: e2.tensor_tensor(
                            out=gw[:, sl, 0:f], in0=gw[:, sl, 0:f], in1=gu[:, 0:f],
                            op=ALU.add))
                    ops_.append(ao)
                else:
                    def alast(sl=slot, f=F, t=tile):
                        wait("vector", s_v, s_v.n)
                        cop("vector", lambda e2: e2.tensor_tensor(
                            out=gw[:, sl, 0:f], in0=gw[:, sl, 0:f], in1=gu[:, 0:f],
                            op=ALU.add), s_v)
                        gath_done_v[t] = s_v.n
                        gather_stage(t)
                    ops_.append(alast)
            self.q = ops_
            return True

        def pump(self, n):
            k = 0
            while k < n:
                if not self.q:
                    if not self._refill():
                        return
                self.q.pop(0)()
                k += 1

        def flush_matrix(self, mat):
            while any(t[0] == mat for t in self.tiles[self.ti:]) or (
                    self.cur is not None and self.cur[0] == mat and self.q):
                if not self.q and not self._refill():
                    return
                while self.q:
                    self.q.pop(0)()

        def done_through(self, idx):
            return self.ti > idx + 1 or (self.ti == idx + 1 and not self.q)

    def serial_v(fn):
        v0 = cop("vector", fn, s_v)
        wait("vector", s_v, v0)
        return v0

    def ln_stats_math_and_bcast():
        wait("vector", s_v, s_v.n)
        mn = serial_v(lambda e: e.tensor_scalar(
            out=stt1[:], in0=stt1[:], scalar1=1.0 / D, scalar2=None, op0=ALU.mult))
        wait("scalar", s_v, mn)
        serial_v(lambda e: e.tensor_scalar(
            out=stt2[:], in0=stt2[:], scalar1=1.0 / D, scalar2=None, op0=ALU.mult))
        serial_v(lambda e: e.tensor_tensor(
            out=stt3[:], in0=stt1[:], in1=stt1[:], op=ALU.mult))
        serial_v(lambda e: e.tensor_tensor(
            out=stt2[:], in0=stt2[:], in1=stt3[:], op=ALU.subtract))
        v = serial_v(lambda e: e.tensor_scalar(
            out=stt2[:], in0=stt2[:], scalar1=EPS, scalar2=None, op0=ALU.add))
        wait("scalar", s_v, v)
        v = cop("scalar", lambda e: e.activation(out=stt3[:], in_=stt2[:],
                                                 func=AF.Ln), s_sc)
        wait("scalar", s_sc, v)
        v = cop("scalar", lambda e: e.activation(out=st_bf[:, 0:T], in_=stt3[:],
                                                 func=AF.Exp, scale=-0.5), s_sc)
        wait("vector", s_sc, v)
        mv = dma("scalar", mean_scr, stt1[0:1, :], s_mb)
        wait("scalar", s_mb, mv)
        mv = dma("scalar", m_b[:], mean_scr.partition_broadcast(128), s_mb)
        wait("tensor", s_sc, v)
        tv = None
        for p in range(4):
            tv = cop("tensor", lambda e, pp=p: e.matmul(
                psum[:, pp * 512:(pp + 1) * 512], ones_r[:],
                st_bf[:, pp * 512:(pp + 1) * 512],
                start=True, stop=True), s_t)
        wait("vector", s_t, tv)
        out_v = cop("vector", lambda e: e.tensor_copy(
            out=rstd_b[:], in_=psum[:, 0:T]), s_v)
        wait("tensor", s_v, out_v)
        return out_v, mv

    pump = GPump(gather_tiles)

    xa_free = {0: None, 1: None}
    xc_free = {0: None, 1: None}
    cast_done = {}
    sq_done = {}
    stat_done = {}
    for kk in range(KC):
        sl = kk % 2
        wait_free("sync", xa_free[sl])
        xv = dma("sync", xa[:, sl, :], x_d[kk * 128:(kk + 1) * 128, :], s_xa[sl])
        wait("vector", s_xa[sl], xv)
        if kk >= 2:
            wait("vector", s_t, stat_done[kk - 2])
        cast_done[kk] = cop("vector", lambda e, q0=sl: e.tensor_copy(
            out=xcbf[:, q0 * 2048:(q0 + 1) * 2048], in_=xa[:, q0, :]), s_v)
        xa_free[sl] = (s_v, cast_done[kk])
        wait("scalar", s_v, cast_done[kk])
        sq_done[kk] = cop("scalar", lambda e, q0=sl: e.activation(
            out=sqbf[:, q0 * 2048:(q0 + 1) * 2048],
            in_=xcbf[:, q0 * 2048:(q0 + 1) * 2048], func=AF.Square), s_sc)
        wait("tensor", s_v, cast_done[kk])
        wait("tensor", s_sc, sq_done[kk])
        for p in range(4):
            cop("tensor", lambda e, q0=sl, pp=p, k0=kk: e.matmul(
                psum[0:1, pp * 512:(pp + 1) * 512], ones_c[:],
                xcbf[:, q0 * 2048 + pp * 512:q0 * 2048 + (pp + 1) * 512],
                start=(k0 == 0), stop=(k0 == KC - 1)))
        for p in range(4):
            stat_done[kk] = cop("tensor", lambda e, q0=sl, pp=p, k0=kk: e.matmul(
                psum[0:1, 2048 + pp * 512:2048 + (pp + 1) * 512], ones_c[:],
                sqbf[:, q0 * 2048 + pp * 512:q0 * 2048 + (pp + 1) * 512],
                start=(k0 == 0), stop=(k0 == KC - 1)), s_t)
        pump.pump(10)
    sqst_t = stat_done[KC - 1]
    wait("vector", s_t, sqst_t)
    v = cop("vector", lambda e: e.tensor_copy(out=stt1[:], in_=psum[0:1, 0:2048]), s_v)
    wait("vector", s_v, v)
    v = cop("vector", lambda e: e.tensor_copy(out=stt2[:], in_=psum[0:1, 2048:4096]), s_v)
    wait("tensor", s_v, v)
    ln1_done, ln1_mb = ln_stats_math_and_bcast()
    ag_done = {"g": None, "u": None, "d": None}

    def trigger_ag(key, agi, ago):
        wait("gpsimd", s_st2[0], s_st2[0].n)
        wait("gpsimd", s_st2[1], s_st2[1].n)
        wait("gpsimd", s_st3[0], s_st3[0].n)
        wait("gpsimd", s_st3[1], s_st3[1].n)
        cop("gpsimd", lambda e, i0=agi, o0=ago: e.collective_compute(
            "AllGather", ALU.bypass, replica_groups=[list(range(CORES))],
            ins=[i0.opt()], outs=[o0.opt()]), s_cc)
        ag_done[key] = s_cc.n

    if DEBUG:
        wait("gpsimd", s_v, ln1_done)
        dma("gpsimd", dbg_rstd, rstd_b[:], s_out)
        pass

    xnc_free = {0: None, 1: None}
    qkv_ev = 0
    for th in range(2):
        for kk in range(KC):
            slot = kk % 2
            wait_free("sync", xw_free[slot])
            xv = dma("sync", xw[:, slot, :],
                     x_d[kk * 128:(kk + 1) * 128, th * 1024:(th + 1) * 1024], s_xw[slot])
            wait("vector", s_xw[slot], xv)
            if th == 0 and kk == 0:
                wait("vector", s_mb, ln1_mb)
            wait_free("vector", xnc_free[slot])
            v0 = cop("vector", lambda e, t0=th, sl=slot: e.tensor_tensor(
                out=xw[:, sl, :], in0=xw[:, sl, :],
                in1=m_b[:, t0 * 1024:(t0 + 1) * 1024], op=ALU.subtract), s_v)
            wait("vector", s_v, v0)
            xn_v = cop("vector", lambda e, t0=th, sl=slot: e.tensor_tensor(
                out=xnc[:, sl, :], in0=xw[:, sl, :],
                in1=rstd_b[:, t0 * 1024:(t0 + 1) * 1024], op=ALU.mult), s_v)
            xw_free[slot] = (s_v, xn_v)
            if DEBUG and th == 0:
                wait("gpsimd", s_v, xn_v)
                dma("gpsimd", dbg_xn[kk], xnc[:, kk % 2, :], s_out)
                _dv = s_out.n
                wait("vector", s_out, _dv)
            wait("tensor", s_v, xn_v)
            if th == 0 and kk == 0:
                wait("tensor", s_v, wqkv_cast)
                wait("tensor", s_v, ln1_done)
            if th == 1 and kk == 0:
                wait("tensor", s_v, qkv_ev)
            qk_t = None
            for wi, wbf in enumerate((wq_bf, wk_bf, wv_bf)):
                for p in range(2):
                    qk_t = cop("tensor", lambda e, w=wbf, k0=kk, wi0=wi, pp=p, sl=slot: e.matmul(
                        psum[:, wi0 * 1024 + pp * 512: wi0 * 1024 + (pp + 1) * 512],
                        w[:, k0, :],
                        xnc[:, sl, pp * 512:(pp + 1) * 512],
                        start=(k0 == 0), stop=(k0 == KC - 1)),
                        s_t if (wi == 2 and p == 1) else None)
            xnc_free[slot] = (s_t, qk_t)
            pump.pump(3)
        wait("vector", s_t, qk_t)
        if DEBUG and th == 0:
            cop("vector", lambda e: e.tensor_copy(out=x1h[:, 0, :], in_=psum[:, 0:1024]), s_v)
            wait("gpsimd", s_v, s_v.n)
            dma("gpsimd", dbg_qp, x1h[:, 0, :], s_out)
            wait("vector", s_out, s_out.n)
        cop("vector", lambda e, t0=th: e.tensor_scalar(
            out=q_bf[:, t0 * 1024:(t0 + 1) * 1024], in0=psum[:, 0:1024],
            scalar1=bqkv[:, 0:1], scalar2=INVSQ, op0=ALU.add, op1=ALU.mult))
        cop("vector", lambda e, t0=th: e.tensor_scalar(
            out=k_bf[:, t0 * 1024:(t0 + 1) * 1024], in0=psum[:, 1024:2048],
            scalar1=bqkv[:, 1:2], scalar2=None, op0=ALU.add))
        qkv_ev = cop("vector", lambda e, t0=th: e.tensor_scalar(
            out=v_bf[:, t0 * 1024:(t0 + 1) * 1024], in0=psum[:, 2048:3072],
            scalar1=bqkv[:, 2:3], scalar2=None, op0=ALU.add), s_v)

    wait("tensor", s_v, qkv_ev)
    ev = 0
    for r0 in range(0, KC, 8):
        tv = None
        for j in range(r0, r0 + 8):
            tv = cop("tensor", lambda e, jj=j, sl=j - r0: e.transpose(
                ps_tr[:, sl * 128:(sl + 1) * 128],
                v_bf[:, jj * 128:(jj + 1) * 128], ident[:]), s_t)
        wait("vector", s_t, tv)
        ev = cop("vector", lambda e, a=r0: e.tensor_copy(
            out=v_tm[:, a:a + 8, :].rearrange("p a b -> p (a b)"),
            in_=ps_tr[:, 0:1024]), s_v)
        if r0 == 0:
            wait("tensor", s_v, ev)

    if DEBUG:
        wait("gpsimd", s_v, qkv_ev)
        dma("gpsimd", dbg_q, q_bf[:], s_out)
        dma("gpsimd", dbg_k, k_bf[:], s_out)
        dma("gpsimd", dbg_v, v_bf[:], s_out)
        wait("gpsimd", s_v, ev)
        dma("gpsimd", dbg_vtm, v_tm.rearrange("p a b -> p (a b)"), s_out)

    prev_tile_v = ev
    for qt in range(KC):
        valid = (qt + 1) * 128
        wait("tensor", s_v, prev_tile_v)
        npc = (valid + 511) // 512
        sc_t = None
        for p in range(npc):
            lo = p * 512
            hi = min(valid, lo + 512)
            sc_t = cop("tensor", lambda e, q0=qt, l0=lo, h0=hi: e.matmul(
                psum[:, l0:h0], q_bf[:, q0 * 128:(q0 + 1) * 128], k_bf[:, l0:h0],
                start=True, stop=True), s_t)
        wait("vector", s_t, sc_t)
        v0 = cop("vector", lambda e, q0=qt: e.tensor_tensor(
            out=psum[:, q0 * 128:(q0 + 1) * 128], in0=psum[:, q0 * 128:(q0 + 1) * 128],
            in1=cmask[:], op=ALU.add), s_v)
        wait("vector", s_v, v0)
        mx_v = cop("vector", lambda e, v0=valid: e.reduce_max(
            mx[:], psum[:, 0:v0], axis=AX.X, negate=True), s_v)
        wait("scalar", s_v, mx_v)
        ex_v = cop("scalar", lambda e, v0=valid: e.activation(
            out=probs[:, 0:v0], in_=psum[:, 0:v0], func=AF.Exp,
            bias=mx[:, 0:1], scale=1.0), s_sc)
        wait("vector", s_sc, ex_v)
        rs_v = cop("vector", lambda e, v0=valid: e.reduce_sum(
            rs[:], probs[:, 0:v0], axis=AX.X), s_v)
        wait("scalar", s_v, rs_v)
        lnv = cop("scalar", lambda e: e.activation(out=rct[:], in_=rs[:],
                                                   func=AF.Ln), s_sc)
        wait("scalar", s_sc, lnv)
        rc_v = cop("scalar", lambda e: e.activation(out=rcp[:], in_=rct[:],
                                                    func=AF.Exp, scale=-1.0), s_sc)
        wait("vector", s_sc, rc_v)
        nrm_v = cop("vector", lambda e, v0=valid: e.tensor_scalar(
            out=probs[:, 0:v0], in0=probs[:, 0:v0], scalar1=rcp[:, 0:1],
            scalar2=None, op0=ALU.mult), s_v)
        if DEBUG and qt == 1:
            wait("gpsimd", s_v, nrm_v)
            dma("gpsimd", dbg_probs1, probs[:, 0:256], s_out)
            dma("gpsimd", dbg_mx1, mx[:], s_out)
            dma("gpsimd", dbg_rs1, rs[:], s_out)
            dbg_t1 = s_out.n
            wait("vector", s_out, dbg_t1)
        wait("tensor", s_v, nrm_v)
        av_t = None
        for r0 in range(0, qt + 1, 8):
            r1 = min(qt + 1, r0 + 8)
            tr_t = None
            for j in range(r0, r1):
                tr_t = cop("tensor", lambda e, jj=j, sl=j - r0: e.transpose(
                    ps_tr[:, sl * 128:(sl + 1) * 128],
                    probs[:, jj * 128:(jj + 1) * 128], ident[:]), s_t)
            wait("vector", s_t, tr_t)
            pev = cop("vector", lambda e, a=r0, b2=r1: e.tensor_copy(
                out=probsT[:, a * 128:b2 * 128], in_=ps_tr[:, 0:(b2 - a) * 128]), s_v)
            wait("tensor", s_v, pev)
            for j in range(r0, r1):
                av_t = cop("tensor", lambda e, jj=j, q0=qt: e.matmul(
                    ps_out[:, :], v_tm[:, jj, :], probsT[:, jj * 128:(jj + 1) * 128],
                    start=(jj == 0), stop=(jj == q0)),
                    s_t if j == qt else None)
        wait("vector", s_t, av_t)
        prev_tile_v = cop("vector", lambda e, q0=qt: e.tensor_copy(
            out=attn[:, q0 * 128:(q0 + 1) * 128], in_=ps_out[:, :]), s_v)
        pump.pump(5)
        if ag_done["g"] is None and pump.done_through(KC // 2 - 1):
            trigger_ag("g", agi_g, ago_g)
        if DEBUG and qt == 1:
            wait("gpsimd", s_v, prev_tile_v)
            dma("gpsimd", dbg_probsT1, probsT[:, 0:256], s_out)
            dbg_t2 = s_out.n
            wait("vector", s_out, dbg_t2)
    attn_v = prev_tile_v
    if DEBUG:
        wait("gpsimd", s_v, attn_v)
        dma("gpsimd", dbg_attn, attn[:], s_out)

    x1wv = {}
    prev_stt = attn_v
    st2_t = 0
    for th in range(2):
        for mm in range(KC):
            slot = mm % 2
            wait_free("sync", xw_free[slot])
            xv = dma("sync", xw[:, slot, :],
                     x_d[mm * 128:(mm + 1) * 128, th * 1024:(th + 1) * 1024], s_xw[slot])
            wait("tensor", s_v, prev_stt)
            if th == 0 and mm == 0:
                wait("tensor", s_v, wo_cast)
            pp_t = None
            for p in range(2):
                pp_t = cop("tensor", lambda e, m0=mm, t0=th, pp=p: e.matmul(
                    psum[:, pp * 512:(pp + 1) * 512], wo_bf[:, m0, :],
                    attn[:, t0 * 1024 + pp * 512:t0 * 1024 + (pp + 1) * 512],
                    start=True, stop=True), s_t if p == 1 else None)
            wait("vector", s_t, pp_t)
            wait("vector", s_xw[slot], xv)
            wait_free("vector", x1h_free[slot])
            if th > 0 or mm > 0:
                wait("vector", s_sc, sq_v)
                wait("vector", s_t, st2_t)
            v0 = cop("vector", lambda e, m0=mm, sl=slot: e.scalar_tensor_tensor(
                out=x1h[:, sl, :], in0=psum[:, 0:1024], scalar=bo_sb[:, m0:m0 + 1],
                in1=xw[:, sl, :], op0=ALU.add, op1=ALU.add), s_v)
            wait("vector", s_v, v0)
            x1_v = cop("vector", lambda e, sl=slot: e.tensor_copy(
                out=xcbf[:, 0:1024], in_=x1h[:, sl, :]), s_v)
            prev_stt = x1_v
            xw_free[slot] = (s_v, x1_v)
            wait("sync", s_v, x1_v)
            x1wv[(th, mm)] = dma(
                "sync", x1_d[mm * 128:(mm + 1) * 128, th * 1024:(th + 1) * 1024],
                x1h[:, slot, :], s_x1w2[slot])
            x1h_free[slot] = (s_x1w2[slot], x1wv[(th, mm)])
            wait("scalar", s_v, x1_v)
            sq_v = cop("scalar", lambda e: e.activation(
                out=sqbf[:, 0:1024], in_=xcbf[:, 0:1024], func=AF.Square), s_sc)
            wait("tensor", s_v, x1_v)
            wait("tensor", s_sc, sq_v)
            if th == 1 and mm == 0:
                wait("tensor", s_v, st_ev)
            for p in range(2):
                cop("tensor", lambda e, m0=mm, pp=p: e.matmul(
                    psum[0:1, 2048 + pp * 512:2048 + (pp + 1) * 512], ones_c[:],
                    xcbf[:, pp * 512:(pp + 1) * 512],
                    start=(m0 == 0), stop=(m0 == KC - 1)))
            for p in range(2):
                st2_t = cop("tensor", lambda e, m0=mm, pp=p: e.matmul(
                    psum[0:1, 3072 + pp * 512:3072 + (pp + 1) * 512], ones_c[:],
                    sqbf[:, pp * 512:(pp + 1) * 512],
                    start=(m0 == 0), stop=(m0 == KC - 1)), s_t)
            pump.pump(5)
            if ag_done["g"] is None and pump.done_through(KC // 2 - 1):
                trigger_ag("g", agi_g, ago_g)
            if ag_done["u"] is None and pump.done_through(KC - 1):
                trigger_ag("u", agi_u, ago_u)
            if ag_done["d"] is None and pump.done_through(KC + BB - 1):
                trigger_ag("d", agi_dn, ago_dn)
        wait("vector", s_t, st2_t)
        v0 = cop("vector", lambda e, t0=th: e.tensor_copy(
            out=stt1[0:1, t0 * 1024:(t0 + 1) * 1024], in_=psum[0:1, 2048:3072]), s_v)
        wait("vector", s_v, v0)
        st_ev = cop("vector", lambda e, t0=th: e.tensor_copy(
            out=stt2[0:1, t0 * 1024:(t0 + 1) * 1024], in_=psum[0:1, 3072:4096]), s_v)
    wait("scalar", s_v, qkv_ev)
    ln2_done, ln2_mb = ln_stats_math_and_bcast()

    pump.flush_matrix("g")
    if ag_done["g"] is None:
        trigger_ag("g", agi_g, ago_g)

    if DEBUG:
        dma("gpsimd", dbg_wg, ago_g, s_out)

    x1_all_written = [(s_x1w2[0], s_x1w2[0].n), (s_x1w2[1], s_x1w2[1].n)]
    ag_u_done = None
    ag_d_done = None
    xn2_done = {}
    xr_free = {0: None}

    def emit_xn2_chunk(tt, kk, buf_ap, slot_key, free_map, q="scalar"):
        for sem0, v0 in x1_all_written:
            wait(q, sem0, v0)
        if free_map is not None:
            wait_free(q, free_map[slot_key])
        sem = s_xw[slot_key]
        xv = dma(q, buf_ap,
                 x1_d[kk * 128:(kk + 1) * 128, tt * 1024:(tt + 1) * 1024], sem)
        wait("vector", sem, xv)
        if kk == 0:
            wait("vector", s_mb, ln2_mb)
        v0 = cop("vector", lambda e, b=buf_ap, t0=tt: e.tensor_tensor(
            out=b, in0=b, in1=m_b[:, t0 * 1024:(t0 + 1) * 1024], op=ALU.subtract), s_v)
        wait("vector", s_v, v0)
        v = cop("vector", lambda e, b=buf_ap, k0=kk, t0=tt: e.tensor_tensor(
            out=xn2[:, k0, :], in0=b,
            in1=rstd_b[:, t0 * 1024:(t0 + 1) * 1024], op=ALU.mult), s_v)
        if free_map is not None:
            free_map[slot_key] = (s_v, v)
        return v

    xr_free = {0: None}
    for tt in range(2):
        for kk in range(KC):
            xn2_done[tt] = emit_xn2_chunk(tt, kk, xw[:, kk % 2, :], kk % 2, xw_free)

        if DEBUG and tt == 0:
            wait("gpsimd", s_v, xn2_done[0])
            dma("gpsimd", dbg_xn2, xn2[:, :, :], s_out)

        wait("sync", s_cc, ag_done["g"])
        silu_done = {}
        for g in range(OC):
            slot = g % 3
            r, b = g % 8, g // 8
            wait_free("sync", wgu_free[slot])
            wl = dma("sync", wgu[slot][:, :, :], ago_g[r, b], s_wgu[slot])
            wait("tensor", s_wgu[slot], wl)
            if g == 0:
                wait("tensor", s_v, xn2_done[tt])
            wait_free("tensor", psB_free[slot])
            g_t = None
            for kk in range(KC):
                for p in range(2):
                    g_t = cop("tensor", lambda e, sl=slot, k0=kk, pp=p: e.matmul(
                        psum[:, sl * 1024 + pp * 512:sl * 1024 + (pp + 1) * 512],
                        wgu[sl][:, k0, :], xn2[:, k0, pp * 512:(pp + 1) * 512],
                        start=(k0 == 0), stop=(k0 == KC - 1)),
                        s_t if (kk == KC - 1 and p == 1) else None)
            wgu_free[slot] = (s_t, g_t)
            wait("scalar", s_t, g_t)
            sv = cop("scalar", lambda e, gg=g, sl=slot: e.activation(
                out=h_sb[:, gg, :], in_=psum[:, sl * 1024:sl * 1024 + 1024],
                func=AF.Silu), s_sc)
            psB_free[slot] = (s_sc, sv)
            if tt == 0:
                pump.pump(7)
                if ag_done["u"] is None and pump.done_through(KC - 1):
                    trigger_ag("u", agi_u, ago_u)
                if ag_done["d"] is None and pump.done_through(KC + BB - 1):
                    trigger_ag("d", agi_dn, ago_dn)
        if tt == 0:
            if ag_done["u"] is None:
                pump.flush_matrix("u")
                trigger_ag("u", agi_u, ago_u)
            if ag_done["d"] is None:
                pump.flush_matrix("d")
                trigger_ag("d", agi_dn, ago_dn)

        wait("sync", s_cc, ag_done["u"])
        for g in range(OC):
            slot = g % 3
            r, b = g % 8, g // 8
            wait_free("sync", wgu_free[slot])
            wl = dma("sync", wgu[slot][:, :, :], ago_u[r, b], s_wgu[slot])
            wait("tensor", s_wgu[slot], wl)
            wait_free("tensor", psB_free[slot])
            g_t = None
            for kk in range(KC):
                for p in range(2):
                    g_t = cop("tensor", lambda e, sl=slot, k0=kk, pp=p: e.matmul(
                        psum[:, sl * 1024 + pp * 512:sl * 1024 + (pp + 1) * 512],
                        wgu[sl][:, k0, :], xn2[:, k0, pp * 512:(pp + 1) * 512],
                        start=(k0 == 0), stop=(k0 == KC - 1)),
                        s_t if (kk == KC - 1 and p == 1) else None)
            wgu_free[slot] = (s_t, g_t)
            wait("vector", s_t, g_t)
            hv = cop("vector", lambda e, gg=g, sl=slot: e.tensor_tensor(
                out=h_sb[:, gg, :], in0=h_sb[:, gg, :],
                in1=psum[:, sl * 1024:sl * 1024 + 1024], op=ALU.mult), s_v)
            psB_free[slot] = (s_v, hv)
        h_done_v = hv

        if DEBUG and tt == 0:
            wait("gpsimd", s_v, h_done_v)
            dma("gpsimd", dbg_h, h_sb[:, :, :], s_out)
            dma("gpsimd", dbg_wd, ago_dn, s_out)
            _hv = s_out.n
            wait("scalar", s_out, _hv)

        wait("sync", s_cc, ag_done["d"])
        for mm in range(KC):
            ws3 = mm % 2
            slot = mm % 2
            ps3 = mm % 3
            wait_free("sync", wdn_free[ws3])
            wl = None
            for r in range(8):
                wl = dma("sync", wdn[ws3][:, r * 6:(r + 1) * 6, :], ago_dn[r, mm], s_wdn[ws3])
            wait_free("scalar", xw_free[slot])
            xv = dma("scalar", xw[:, slot, :],
                     x1_d[mm * 128:(mm + 1) * 128, tt * 1024:(tt + 1) * 1024], s_xw[slot])
            wait("tensor", s_wdn[ws3], wl)
            if mm == 0:
                wait("tensor", s_v, h_done_v)
            wait_free("tensor", psB_free[ps3])
            d_t = None
            for g in range(OC):
                wslot = (g % 8) * 6 + g // 8
                for p in range(2):
                    d_t = cop("tensor", lambda e, w3=ws3, sl=ps3, gg=g, ws=wslot, pp=p: e.matmul(
                        psum[:, sl * 1024 + pp * 512:sl * 1024 + (pp + 1) * 512],
                        wdn[w3][:, ws, :], h_sb[:, gg, pp * 512:(pp + 1) * 512],
                        start=(gg == 0), stop=(gg == OC - 1)),
                        s_t if (g == OC - 1 and p == 1) else None)
            wdn_free[ws3] = (s_t, d_t)
            wait("vector", s_t, d_t)
            wait("vector", s_xw[slot], xv)
            wait_free("vector", ysb_free[0])
            yv = cop("vector", lambda e, sl=slot, p3=ps3: e.tensor_tensor(
                out=y_sb[:, 0, :], in0=psum[:, p3 * 1024:p3 * 1024 + 1024],
                in1=xw[:, sl, :], op=ALU.add), s_v)
            psB_free[ps3] = (s_v, yv)
            xw_free[slot] = (s_v, yv)
            wait("gpsimd", s_v, yv)
            ov = dma("gpsimd", y_d[mm * 128:(mm + 1) * 128, tt * 1024:(tt + 1) * 1024],
                     y_sb[:, 0, :], s_out)
            ysb_free[0] = (s_out, ov)



    wait("sync", s_out, s_out.n)
    wait("gpsimd", s_out, s_out.n)

    with nc.Block() as block:
        @block.sync
        def _(e):
            for f in ops["sync"]:
                f(e)

        @block.gpsimd
        def _(e):
            for f in ops["gpsimd"]:
                f(e)

        @block.vector
        def _(e):
            for f in ops["vector"]:
                f(e)

        @block.tensor
        def _(e):
            for f in ops["tensor"]:
                f(e)

        @block.scalar
        def _(e):
            for f in ops["scalar"]:
                f(e)

    es.close()
    return nc


def _prep_inputs(L0_attn_x, Wq, bq, Wk, bk, Wv, bv, Wo, bo, lut, gidx, uidx, didx):
    x = np.asarray(L0_attn_x, np.float32)
    gidx = np.asarray(gidx, np.int32)
    uidx = np.asarray(uidx, np.int32)
    didx = np.asarray(didx, np.int32)
    in_maps = []
    for c in range(CORES):
        gblocks, ublocks, dblocks = [], [], []
        for b in range(BB):
            g = b * 8 + c
            if g >= OC:
                g = OC - 1
            gblocks.append(gidx[g * 128:(g + 1) * 128, :].T)
            ublocks.append(uidx[g * 128:(g + 1) * 128, :].T)
            dblocks.append(didx[:, g * 128:(g + 1) * 128].T)
        in_maps.append({
            "x": np.ascontiguousarray(x[c].T),
            "wq": np.ascontiguousarray(np.asarray(Wq, np.float32)),
            "wk": np.ascontiguousarray(np.asarray(Wk, np.float32)),
            "wv": np.ascontiguousarray(np.asarray(Wv, np.float32)),
            "wo": np.ascontiguousarray(np.asarray(Wo, np.float32)),
            "bq": np.ascontiguousarray(np.asarray(bq, np.float32).reshape(HD, 1)),
            "bk": np.ascontiguousarray(np.asarray(bk, np.float32).reshape(HD, 1)),
            "bv": np.ascontiguousarray(np.asarray(bv, np.float32).reshape(HD, 1)),
            "bo": np.ascontiguousarray(np.asarray(bo, np.float32).reshape(D, 1)),
            "lut": np.ascontiguousarray(np.asarray(lut, np.float32)),
            "gidx": np.ascontiguousarray(np.concatenate(gblocks, axis=1)),
            "uidx": np.ascontiguousarray(np.concatenate(ublocks, axis=1)),
            "didx": np.ascontiguousarray(np.concatenate(dblocks, axis=0)),
        })
    return in_maps


def kernel(L0_attn_x, L0_attn_Wq, L0_attn_bq, L0_attn_Wk, L0_attn_bk,
           L0_attn_Wv, L0_attn_bv, L0_attn_Wo, L0_attn_bo,
           L0_ffn_lut, L0_ffn_gate_idx, L0_ffn_up_idx, L0_ffn_down_idx):
    in_maps = _prep_inputs(L0_attn_x, L0_attn_Wq, L0_attn_bq, L0_attn_Wk,
                           L0_attn_bk, L0_attn_Wv, L0_attn_bv, L0_attn_Wo,
                           L0_attn_bo, L0_ffn_lut, L0_ffn_gate_idx,
                           L0_ffn_up_idx, L0_ffn_down_idx)
    nc = build()
    res = run_bass_kernel_spmd(nc, in_maps, core_ids=list(range(CORES)))
    out = np.stack([np.ascontiguousarray(r["y"].T) for r in res.results])
    return out.astype(np.float32)


# version 34
# speedup vs baseline: 1.1529x; 1.0015x over previous
import contextlib
import numpy as np
import concourse.bass as bass
import concourse.mybir as mybir
from concourse.bass_utils import run_bass_kernel_spmd

F32 = mybir.dt.float32
BF16 = mybir.dt.bfloat16
I32 = mybir.dt.int32
AF = mybir.ActivationFunctionType
ALU = mybir.AluOpType
AX = mybir.AxisListType

CORES = 8
D = 2048
T = 2048
HD = 128
O = 5504
KC = D // 128
OC = 43
BB = 6
NEG = -1.0e30
EPS = 1e-5
INVSQ = 1.0 / float(np.sqrt(128.0))
DEBUG = False


class SemC:
    def __init__(self, nc, es, name):
        self.h = es.enter_context(nc.semaphore(name))
        self.n = 0


def build():
    nc = bass.Bass("TRN2", target_bir_lowering=False, debug=False)

    x_d = nc.dram_tensor("x", [D, T], F32, kind="ExternalInput").ap()
    wq_d = nc.dram_tensor("wq", [D, HD], F32, kind="ExternalInput").ap()
    wk_d = nc.dram_tensor("wk", [D, HD], F32, kind="ExternalInput").ap()
    wv_d = nc.dram_tensor("wv", [D, HD], F32, kind="ExternalInput").ap()
    wo_d = nc.dram_tensor("wo", [HD, D], F32, kind="ExternalInput").ap()
    bq_d = nc.dram_tensor("bq", [HD, 1], F32, kind="ExternalInput").ap()
    bk_d = nc.dram_tensor("bk", [HD, 1], F32, kind="ExternalInput").ap()
    bv_d = nc.dram_tensor("bv", [HD, 1], F32, kind="ExternalInput").ap()
    bo_d = nc.dram_tensor("bo", [D, 1], F32, kind="ExternalInput").ap()
    lut_d = nc.dram_tensor("lut", [16], F32, kind="ExternalInput").ap()
    gidx_d = nc.dram_tensor("gidx", [D, BB * 128], I32, kind="ExternalInput").ap()
    uidx_d = nc.dram_tensor("uidx", [D, BB * 128], I32, kind="ExternalInput").ap()
    didx_d = nc.dram_tensor("didx", [BB * 128, D], I32, kind="ExternalInput").ap()
    y_d = nc.dram_tensor("y", [D, T], F32, kind="ExternalOutput").ap()
    if DEBUG:
        dbg_q = nc.dram_tensor("dbg_q", [128, T], BF16, kind="ExternalOutput").ap()
        dbg_k = nc.dram_tensor("dbg_k", [128, T], BF16, kind="ExternalOutput").ap()
        dbg_v = nc.dram_tensor("dbg_v", [128, T], BF16, kind="ExternalOutput").ap()
        dbg_vtm = nc.dram_tensor("dbg_vtm", [128, T], BF16, kind="ExternalOutput").ap()
        dbg_probs1 = nc.dram_tensor("dbg_probs1", [128, 256], BF16, kind="ExternalOutput").ap()
        dbg_probsT1 = nc.dram_tensor("dbg_probsT1", [128, 256], BF16, kind="ExternalOutput").ap()
        dbg_mx1 = nc.dram_tensor("dbg_mx1", [128, 1], F32, kind="ExternalOutput").ap()
        dbg_rs1 = nc.dram_tensor("dbg_rs1", [128, 1], F32, kind="ExternalOutput").ap()
        dbg_attn = nc.dram_tensor("dbg_attn", [128, T], BF16, kind="ExternalOutput").ap()
        dbg_rstd = nc.dram_tensor("dbg_rstd", [128, T], BF16, kind="ExternalOutput").ap()
        dbg_xn = nc.dram_tensor("dbg_xn", [16, 128, 1024], BF16, kind="ExternalOutput").ap()
        dbg_wg = nc.dram_tensor("dbg_wg", [CORES, BB, 128, KC, 128], BF16, kind="ExternalOutput").ap()
        dbg_wd = nc.dram_tensor("dbg_wd", [CORES, KC, 128, BB, 128], BF16, kind="ExternalOutput").ap()
        dbg_xn2 = nc.dram_tensor("dbg_xn2", [128, KC, 1024], BF16, kind="ExternalOutput").ap()
        dbg_h = nc.dram_tensor("dbg_h", [128, OC, 1024], BF16, kind="ExternalOutput").ap()
        dbg_qp = nc.dram_tensor("dbg_qp", [128, 1024], F32, kind="ExternalOutput").ap()


    x1_d = nc.dram_tensor("x1scr", [D, T], F32).ap()
    mean_scr = nc.dram_tensor("mean_scr", [1, T], F32).ap()
    dumi = nc.dram_tensor("dumi", [128, 16], F32).ap()
    dumo = nc.dram_tensor("dumo", [8, 128, 16], F32, addr_space="Shared").ap()
    agi_g = nc.dram_tensor("agi_g", [BB, 128, KC, 128], BF16).ap()
    agi_u = nc.dram_tensor("agi_u", [BB, 128, KC, 128], BF16).ap()
    agi_dn = nc.dram_tensor("agi_dn", [KC, 128, BB, 128], BF16).ap()
    ago_g = nc.dram_tensor("ago_g", [CORES, BB, 128, KC, 128], BF16,
                           addr_space="Shared").ap()
    ago_u = nc.dram_tensor("ago_u", [CORES, BB, 128, KC, 128], BF16,
                           addr_space="Shared").ap()
    ago_dn = nc.dram_tensor("ago_dn", [CORES, KC, 128, BB, 128], BF16,
                            addr_space="Shared").ap()

    es = contextlib.ExitStack()
    sb = lambda name, shape, dt: es.enter_context(nc.sbuf_tensor(name, shape, dt))

    lutb = sb("lutb", [128, 16], F32)
    iota_p = sb("iota_p", [128, 1], F32)
    ones_c = sb("ones_c", [128, 1], BF16)
    ones_r = sb("ones_r", [1, 128], BF16)
    rstd_b = sb("rstd_b", [128, T], BF16)
    m_b = sb("m_b", [128, T], F32)
    gi = sb("gi", [128, 2048], I32)
    gibf = sb("gibf", [128, 2048], BF16)
    gu = sb("gu", [128, 2048], BF16)
    gw = sb("gw", [128, 2, 2048], BF16)
    xw = sb("xw", [128, 2, 1024], F32)
    y_sb = sb("y_sb", [128, 1, 1024], F32)
    bqkv = sb("bqkv", [128, 3], F32)
    bo_sb = sb("bo_sb", [128, KC], F32)

    base = nc._sbuf_addr_for_side("left")

    class Region:
        def __init__(self, off):
            self.off = off

        def take(self, name, shape, dt):
            nbytes = int(np.prod(shape[1:])) * mybir.dt.size(dt)
            t = nc.alloc_sbuf_tensor_at(name, shape, dt, offset=self.off)
            self.off += (nbytes + 63) // 64 * 64
            assert self.off <= nc.SBUF_PARTITION_SIZE_BYTES - 1024, (name, self.off)
            return t.ap()

    A = Region(base)
    iota_f = A.take("iota_f", [128, 128], F32)
    ident = A.take("ident", [128, 128], BF16)
    cmask = A.take("cmask", [128, 128], F32)
    stt1 = A.take("stt1", [1, T], F32)
    stt2 = A.take("stt2", [1, T], F32)
    stt3 = A.take("stt3", [1, T], F32)
    st_bf = A.take("st_bf", [1, 2 * T], BF16)
    wqkv_f = A.take("wqkv_f", [128, KC, 128], F32)
    wq_bf = A.take("wq_bf", [128, KC, 128], BF16)
    wk_bf = A.take("wk_bf", [128, KC, 128], BF16)
    wv_bf = A.take("wv_bf", [128, KC, 128], BF16)
    wo_bf = A.take("wo_bf", [128, KC, 128], BF16)
    q_bf = A.take("q_bf", [128, T], BF16)
    k_bf = A.take("k_bf", [128, T], BF16)
    v_bf = A.take("v_bf", [128, T], BF16)
    v_tm = A.take("v_tm", [128, KC, 128], BF16)
    probs = A.take("probs", [128, T], BF16)
    probsT = A.take("probsT", [128, T], BF16)
    attn = A.take("attn", [128, T], BF16)
    xnc = A.take("xnc", [128, 2, 1024], BF16)
    xcbf = A.take("xcbf", [128, 4096], BF16)
    sqbf = A.take("sqbf", [128, 4096], BF16)
    x1h = A.take("x1h", [128, 2, 1024], F32)
    xa = A.take("xa", [128, 2, 2048], F32)
    mx = A.take("mx", [128, 1], F32)
    rs = A.take("rs", [128, 1], F32)
    rcp = A.take("rcp", [128, 1], F32)
    rct = A.take("rct", [128, 1], F32)

    B = Region(base)
    h_sb = B.take("h_sb", [128, OC, 1024], BF16)
    xn2 = B.take("xn2", [128, KC, 1024], BF16)
    wgu = [B.take(f"wgu{i}", [128, KC, 128], BF16) for i in range(3)]
    wdn = [B.take(f"wdn{i}", [128, 48, 128], BF16) for i in range(2)]

    psum = es.enter_context(nc.psum_tensor("psum", [128, 4096], F32))
    ps_tr = psum[:, 3072:3584].bitcast(BF16)
    ps_out = psum[:, 3584:3712]

    s_ld = SemC(nc, es, "s_ld")
    s_xa = [SemC(nc, es, f"s_xa{i}") for i in range(2)]
    s_xw = [SemC(nc, es, f"s_xw{i}") for i in range(2)]
    s_dum = SemC(nc, es, "s_dum")
    s_gi2 = [SemC(nc, es, f"s_gi{i}") for i in range(2)]
    s_st2 = [SemC(nc, es, f"s_st{i}") for i in range(2)]
    s_st3 = [SemC(nc, es, f"s_su{i}") for i in range(2)]
    s_wgu = [SemC(nc, es, f"s_wgu{i}") for i in range(3)]
    s_wdn = [SemC(nc, es, f"s_wdn{i}") for i in range(3)]
    s_out = SemC(nc, es, "s_out")
    s_x1w2 = [SemC(nc, es, f"s_x1w{i}") for i in range(2)]
    s_mb = SemC(nc, es, "s_mb")
    s_cc = SemC(nc, es, "s_cc")
    s_v = SemC(nc, es, "s_v")
    s_t = SemC(nc, es, "s_t")
    s_sc = SemC(nc, es, "s_sc")
    s_g = SemC(nc, es, "s_g")
    s_bar = SemC(nc, es, "s_bar")

    ops = {"sync": [], "gpsimd": [], "vector": [], "tensor": [], "scalar": []}

    def op(eng, fn):
        ops[eng].append(fn)

    def wait(eng, s, val):
        if val > 0:
            ops[eng].append(lambda e, h=s.h, v=val: e.wait_ge(h, v))

    def dma(eng, out, in_, s):
        s.n += 16
        ops[eng].append(lambda e, o=out, i=in_, h=s.h: e.dma_start(out=o, in_=i).then_inc(h, 16))
        return s.n

    def cop(eng, fn, s=None):
        if s is None:
            ops[eng].append(lambda e, f=fn: f(e))
            return None
        s.n += 1
        ops[eng].append(lambda e, f=fn, h=s.h: f(e).then_inc(h, 1))
        return s.n

    xw_free = {0: None, 1: None}
    x1h_free = {0: None, 1: None}
    ysb_free = {0: None, 1: None}
    psB_free = {0: None, 1: None, 2: None}
    wgu_free = {0: None, 1: None, 2: None}
    wdn_free = {0: None, 1: None}

    def wait_free(eng, entry):
        if entry is not None:
            wait(eng, entry[0], entry[1])

    dma("sync", lutb[:], lut_d.partition_broadcast(128), s_ld)
    lut_loaded = s_ld.n
    dma("sync", bqkv[:, 0:1], bq_d, s_ld)
    dma("sync", bqkv[:, 1:2], bk_d, s_ld)
    dma("sync", bqkv[:, 2:3], bv_d, s_ld)
    for mm in range(KC):
        dma("sync", bo_sb[:, mm:mm + 1], bo_d[mm * 128:(mm + 1) * 128, :], s_ld)
    small_ld = s_ld.n
    cop("gpsimd", lambda e: e.dma_start(out=dumi, in_=lut_d.partition_broadcast(128)).then_inc(s_dum.h, 16))
    s_dum.n += 16
    wait("gpsimd", s_dum, s_dum.n)
    cop("gpsimd", lambda e: e.collective_compute(
        "AllGather", ALU.bypass, replica_groups=[list(range(CORES))],
        ins=[dumi.opt()], outs=[dumo.opt()]), s_cc)
    cop("gpsimd", lambda e: e.iota(iota_p[:], pattern=[[0, 1]], channel_multiplier=1,
                                   allow_small_or_imprecise_dtypes=True))
    cop("gpsimd", lambda e: e.iota(iota_f[:], pattern=[[1, 128]], channel_multiplier=0,
                                   allow_small_or_imprecise_dtypes=True), s_g)
    wait("vector", s_g, s_g.n)
    cop("vector", lambda e: e.tensor_scalar(out=ident[:], in0=iota_f[:],
                                            scalar1=iota_p[:, 0:1], scalar2=None,
                                            op0=ALU.is_equal))
    cop("vector", lambda e: e.tensor_scalar(out=cmask[:], in0=iota_f[:],
                                            scalar1=iota_p[:, 0:1], scalar2=NEG,
                                            op0=ALU.is_gt, op1=ALU.mult))
    cop("vector", lambda e: e.memset(ones_c[:], 1.0))
    cop("vector", lambda e: e.memset(ones_r[:], 1.0), s_v)

    wcast = []
    for w_d, w_bf, along_rows in ((wq_d, wq_bf, True), (wk_d, wk_bf, True),
                                  (wv_d, wv_bf, True), (wo_d, wo_bf, False)):
        if wcast:
            wait("sync", s_v, wcast[-1])
        for kk in range(KC):
            src = (w_d[kk * 128:(kk + 1) * 128, :] if along_rows
                   else w_d[:, kk * 128:(kk + 1) * 128])
            v = dma("sync", wqkv_f[:, kk, :], src, s_ld)
        wait("vector", s_ld, v)
        wcast.append(cop("vector", lambda e, b=w_bf: e.tensor_copy(
            out=b[:, :, :], in_=wqkv_f[:, :, :]), s_v))
    wqkv_cast, wo_cast = wcast[2], wcast[3]

    gather_tiles = ([("g", kk) for kk in range(0, KC, 2)]
                    + [("u", kk) for kk in range(0, KC, 2)]
                    + [("d", b) for b in range(BB)])
    g_seq = {t: i for i, t in enumerate(gather_tiles)}
    stage_done_v = {}
    gath_done_v = {}
    cast_done_v = {}
    pump = None

    def gather_idx_load(tile):
        seq = g_seq[tile]
        if seq >= 1:
            wait("gpsimd", s_v, cast_done_v[gather_tiles[seq - 1]])
        if tile[0] in ("g", "u"):
            idx_d = gidx_d if tile[0] == "g" else uidx_d
            kk = tile[1]
            v0 = dma("gpsimd", gi[:, 0:768],
                     idx_d[kk * 128:(kk + 1) * 128, :], s_gi2[0])
            v1 = dma("gpsimd", gi[:, 768:1536],
                     idx_d[(kk + 1) * 128:(kk + 2) * 128, :], s_gi2[1])
            return [(s_gi2[0], v0), (s_gi2[1], v1)]
        _, b = tile
        v0 = dma("gpsimd", gi[:, 0:2048],
                 didx_d[b * 128:(b + 1) * 128, :], s_gi2[0])
        return [(s_gi2[0], v0)]

    def gather_stage(tile):
        slot = g_seq[tile] % 2
        wait("gpsimd", s_v, gath_done_v[tile])
        if tile[0] in ("g", "u"):
            kk = tile[1]
            agi = agi_g if tile[0] == "g" else agi_u
            done = []
            for h in range(2):
                sem = (s_st2 if h == 0 else s_st3)[slot]
                dst = agi[:, :, kk + h, :].transpose([1, 0, 2])
                src = gw[:, slot, h * 768:(h + 1) * 768].rearrange(
                    "p (b j) -> p b j", b=BB)
                done.append((sem, dma("gpsimd", dst, src, sem)))
        else:
            _, b = tile
            done = []
            for hf in range(2):
                sem = (s_st2 if hf == 0 else s_st3)[slot]
                dst = agi_dn[hf * 8:(hf + 1) * 8, :, b, :].transpose([1, 0, 2])
                src = gw[:, slot, hf * 1024:(hf + 1) * 1024].rearrange(
                    "p (m j) -> p m j", m=8)
                done.append((sem, dma("gpsimd", dst, src, sem)))
        stage_done_v[tile] = done

    class GPump:
        def __init__(self, tiles):
            self.tiles = list(tiles)
            self.ti = 0
            self.q = []
            self.cur = None

        def _refill(self):
            if self.ti >= len(self.tiles):
                return False
            tile = self.tiles[self.ti]
            self.ti += 1
            self.cur = tile
            seq = g_seq[tile]
            slot = seq % 2
            F = 2048 if tile[0] == "d" else 1536
            iv = gather_idx_load(tile)
            ops_ = []
            def first(e=None, f=F, t=tile, iv=iv, seq=seq):
                for sem0, v0 in iv:
                    wait("vector", sem0, v0)
                if seq >= 2:
                    for sem0, v0 in stage_done_v[gather_tiles[seq - 2]]:
                        wait("vector", sem0, v0)
                v0 = cop("vector", lambda e2, f=f: e2.tensor_copy(
                    out=gibf[:, 0:f], in_=gi[:, 0:f]), s_v)
                cast_done_v[t] = v0
                wait("vector", s_v, v0)
            ops_.append(first)

            def sv_op(fn):
                wait("vector", s_v, s_v.n)
                cop("vector", fn, s_v)

            def z0(sl=slot, f=F):
                sv_op(lambda e2: e2.tensor_scalar(
                    out=gw[:, sl, 0:f], in0=gibf[:, 0:f], scalar1=0.0,
                    scalar2=lutb[:, 0:1], op0=ALU.is_equal, op1=ALU.mult))
            ops_.append(z0)
            for j in range(1, 16):
                def uo(jj=j, f=F):
                    sv_op(lambda e2: e2.tensor_scalar(
                        out=gu[:, 0:f], in0=gibf[:, 0:f], scalar1=float(jj),
                        scalar2=lutb[:, jj:jj + 1], op0=ALU.is_equal, op1=ALU.mult))
                ops_.append(uo)
                if j < 15:
                    def ao(sl=slot, f=F):
                        sv_op(lambda e2: e2.tensor_tensor(
                            out=gw[:, sl, 0:f], in0=gw[:, sl, 0:f], in1=gu[:, 0:f],
                            op=ALU.add))
                    ops_.append(ao)
                else:
                    def alast(sl=slot, f=F, t=tile):
                        wait("vector", s_v, s_v.n)
                        cop("vector", lambda e2: e2.tensor_tensor(
                            out=gw[:, sl, 0:f], in0=gw[:, sl, 0:f], in1=gu[:, 0:f],
                            op=ALU.add), s_v)
                        gath_done_v[t] = s_v.n
                        gather_stage(t)
                    ops_.append(alast)
            self.q = ops_
            return True

        def pump(self, n):
            k = 0
            while k < n:
                if not self.q:
                    if not self._refill():
                        return
                self.q.pop(0)()
                k += 1

        def flush_matrix(self, mat):
            while any(t[0] == mat for t in self.tiles[self.ti:]) or (
                    self.cur is not None and self.cur[0] == mat and self.q):
                if not self.q and not self._refill():
                    return
                while self.q:
                    self.q.pop(0)()

        def done_through(self, idx):
            return self.ti > idx + 1 or (self.ti == idx + 1 and not self.q)

    def serial_v(fn):
        v0 = cop("vector", fn, s_v)
        wait("vector", s_v, v0)
        return v0

    def ln_stats_math_and_bcast():
        wait("vector", s_v, s_v.n)
        mn = serial_v(lambda e: e.tensor_scalar(
            out=stt1[:], in0=stt1[:], scalar1=1.0 / D, scalar2=None, op0=ALU.mult))
        wait("scalar", s_v, mn)
        serial_v(lambda e: e.tensor_scalar(
            out=stt2[:], in0=stt2[:], scalar1=1.0 / D, scalar2=None, op0=ALU.mult))
        serial_v(lambda e: e.tensor_tensor(
            out=stt3[:], in0=stt1[:], in1=stt1[:], op=ALU.mult))
        serial_v(lambda e: e.tensor_tensor(
            out=stt2[:], in0=stt2[:], in1=stt3[:], op=ALU.subtract))
        v = serial_v(lambda e: e.tensor_scalar(
            out=stt2[:], in0=stt2[:], scalar1=EPS, scalar2=None, op0=ALU.add))
        wait("scalar", s_v, v)
        v = cop("scalar", lambda e: e.activation(out=stt3[:], in_=stt2[:],
                                                 func=AF.Ln), s_sc)
        wait("scalar", s_sc, v)
        v = cop("scalar", lambda e: e.activation(out=st_bf[:, 0:T], in_=stt3[:],
                                                 func=AF.Exp, scale=-0.5), s_sc)
        wait("vector", s_sc, v)
        mv = dma("scalar", mean_scr, stt1[0:1, :], s_mb)
        wait("scalar", s_mb, mv)
        mv = dma("scalar", m_b[:], mean_scr.partition_broadcast(128), s_mb)
        wait("tensor", s_sc, v)
        tv = None
        for p in range(4):
            tv = cop("tensor", lambda e, pp=p: e.matmul(
                psum[:, pp * 512:(pp + 1) * 512], ones_r[:],
                st_bf[:, pp * 512:(pp + 1) * 512],
                start=True, stop=True), s_t)
        wait("vector", s_t, tv)
        out_v = cop("vector", lambda e: e.tensor_copy(
            out=rstd_b[:], in_=psum[:, 0:T]), s_v)
        wait("tensor", s_v, out_v)
        return out_v, mv

    pump = GPump(gather_tiles)

    xa_free = {0: None, 1: None}
    xc_free = {0: None, 1: None}
    cast_done = {}
    sq_done = {}
    stat_done = {}
    for kk in range(KC):
        sl = kk % 2
        wait_free("sync", xa_free[sl])
        xv = dma("sync", xa[:, sl, :], x_d[kk * 128:(kk + 1) * 128, :], s_xa[sl])
        wait("vector", s_xa[sl], xv)
        if kk >= 2:
            wait("vector", s_t, stat_done[kk - 2])
        cast_done[kk] = cop("vector", lambda e, q0=sl: e.tensor_copy(
            out=xcbf[:, q0 * 2048:(q0 + 1) * 2048], in_=xa[:, q0, :]), s_v)
        xa_free[sl] = (s_v, cast_done[kk])
        wait("scalar", s_v, cast_done[kk])
        sq_done[kk] = cop("scalar", lambda e, q0=sl: e.activation(
            out=sqbf[:, q0 * 2048:(q0 + 1) * 2048],
            in_=xcbf[:, q0 * 2048:(q0 + 1) * 2048], func=AF.Square), s_sc)
        wait("tensor", s_v, cast_done[kk])
        wait("tensor", s_sc, sq_done[kk])
        for p in range(4):
            cop("tensor", lambda e, q0=sl, pp=p, k0=kk: e.matmul(
                psum[0:1, pp * 512:(pp + 1) * 512], ones_c[:],
                xcbf[:, q0 * 2048 + pp * 512:q0 * 2048 + (pp + 1) * 512],
                start=(k0 == 0), stop=(k0 == KC - 1)))
        for p in range(4):
            stat_done[kk] = cop("tensor", lambda e, q0=sl, pp=p, k0=kk: e.matmul(
                psum[0:1, 2048 + pp * 512:2048 + (pp + 1) * 512], ones_c[:],
                sqbf[:, q0 * 2048 + pp * 512:q0 * 2048 + (pp + 1) * 512],
                start=(k0 == 0), stop=(k0 == KC - 1)), s_t)
        pump.pump(10)
    sqst_t = stat_done[KC - 1]
    wait("vector", s_t, sqst_t)
    v = cop("vector", lambda e: e.tensor_copy(out=stt1[:], in_=psum[0:1, 0:2048]), s_v)
    wait("vector", s_v, v)
    v = cop("vector", lambda e: e.tensor_copy(out=stt2[:], in_=psum[0:1, 2048:4096]), s_v)
    wait("tensor", s_v, v)
    ln1_done, ln1_mb = ln_stats_math_and_bcast()
    ag_done = {"g": None, "u": None, "d": None}

    def trigger_ag(key, agi, ago):
        wait("gpsimd", s_st2[0], s_st2[0].n)
        wait("gpsimd", s_st2[1], s_st2[1].n)
        wait("gpsimd", s_st3[0], s_st3[0].n)
        wait("gpsimd", s_st3[1], s_st3[1].n)
        cop("gpsimd", lambda e, i0=agi, o0=ago: e.collective_compute(
            "AllGather", ALU.bypass, replica_groups=[list(range(CORES))],
            ins=[i0.opt()], outs=[o0.opt()]), s_cc)
        ag_done[key] = s_cc.n

    if DEBUG:
        wait("gpsimd", s_v, ln1_done)
        dma("gpsimd", dbg_rstd, rstd_b[:], s_out)
        pass

    xnc_free = {0: None, 1: None}
    qkv_ev = 0
    for th in range(2):
        for kk in range(KC):
            slot = kk % 2
            wait_free("sync", xw_free[slot])
            xv = dma("sync", xw[:, slot, :],
                     x_d[kk * 128:(kk + 1) * 128, th * 1024:(th + 1) * 1024], s_xw[slot])
            wait("vector", s_xw[slot], xv)
            if th == 0 and kk == 0:
                wait("vector", s_mb, ln1_mb)
            wait_free("vector", xnc_free[slot])
            v0 = cop("vector", lambda e, t0=th, sl=slot: e.tensor_tensor(
                out=xw[:, sl, :], in0=xw[:, sl, :],
                in1=m_b[:, t0 * 1024:(t0 + 1) * 1024], op=ALU.subtract), s_v)
            wait("vector", s_v, v0)
            xn_v = cop("vector", lambda e, t0=th, sl=slot: e.tensor_tensor(
                out=xnc[:, sl, :], in0=xw[:, sl, :],
                in1=rstd_b[:, t0 * 1024:(t0 + 1) * 1024], op=ALU.mult), s_v)
            xw_free[slot] = (s_v, xn_v)
            if DEBUG and th == 0:
                wait("gpsimd", s_v, xn_v)
                dma("gpsimd", dbg_xn[kk], xnc[:, kk % 2, :], s_out)
                _dv = s_out.n
                wait("vector", s_out, _dv)
            wait("tensor", s_v, xn_v)
            if th == 0 and kk == 0:
                wait("tensor", s_v, wqkv_cast)
                wait("tensor", s_v, ln1_done)
            if th == 1 and kk == 0:
                wait("tensor", s_v, qkv_ev)
            qk_t = None
            for wi, wbf in enumerate((wq_bf, wk_bf, wv_bf)):
                for p in range(2):
                    qk_t = cop("tensor", lambda e, w=wbf, k0=kk, wi0=wi, pp=p, sl=slot: e.matmul(
                        psum[:, wi0 * 1024 + pp * 512: wi0 * 1024 + (pp + 1) * 512],
                        w[:, k0, :],
                        xnc[:, sl, pp * 512:(pp + 1) * 512],
                        start=(k0 == 0), stop=(k0 == KC - 1)),
                        s_t if (wi == 2 and p == 1) else None)
            xnc_free[slot] = (s_t, qk_t)
            pump.pump(3)
        wait("vector", s_t, qk_t)
        if DEBUG and th == 0:
            cop("vector", lambda e: e.tensor_copy(out=x1h[:, 0, :], in_=psum[:, 0:1024]), s_v)
            wait("gpsimd", s_v, s_v.n)
            dma("gpsimd", dbg_qp, x1h[:, 0, :], s_out)
            wait("vector", s_out, s_out.n)
        cop("vector", lambda e, t0=th: e.tensor_scalar(
            out=q_bf[:, t0 * 1024:(t0 + 1) * 1024], in0=psum[:, 0:1024],
            scalar1=bqkv[:, 0:1], scalar2=INVSQ, op0=ALU.add, op1=ALU.mult))
        cop("vector", lambda e, t0=th: e.tensor_scalar(
            out=k_bf[:, t0 * 1024:(t0 + 1) * 1024], in0=psum[:, 1024:2048],
            scalar1=bqkv[:, 1:2], scalar2=None, op0=ALU.add))
        qkv_ev = cop("vector", lambda e, t0=th: e.tensor_scalar(
            out=v_bf[:, t0 * 1024:(t0 + 1) * 1024], in0=psum[:, 2048:3072],
            scalar1=bqkv[:, 2:3], scalar2=None, op0=ALU.add), s_v)

    wait("tensor", s_v, qkv_ev)
    ev = 0
    for r0 in range(0, KC, 8):
        tv = None
        for j in range(r0, r0 + 8):
            tv = cop("tensor", lambda e, jj=j, sl=j - r0: e.transpose(
                ps_tr[:, sl * 128:(sl + 1) * 128],
                v_bf[:, jj * 128:(jj + 1) * 128], ident[:]), s_t)
        wait("vector", s_t, tv)
        ev = cop("vector", lambda e, a=r0: e.tensor_copy(
            out=v_tm[:, a:a + 8, :].rearrange("p a b -> p (a b)"),
            in_=ps_tr[:, 0:1024]), s_v)
        if r0 == 0:
            wait("tensor", s_v, ev)

    if DEBUG:
        wait("gpsimd", s_v, qkv_ev)
        dma("gpsimd", dbg_q, q_bf[:], s_out)
        dma("gpsimd", dbg_k, k_bf[:], s_out)
        dma("gpsimd", dbg_v, v_bf[:], s_out)
        wait("gpsimd", s_v, ev)
        dma("gpsimd", dbg_vtm, v_tm.rearrange("p a b -> p (a b)"), s_out)

    prev_tile_v = ev
    for qt in range(KC):
        valid = (qt + 1) * 128
        wait("tensor", s_v, prev_tile_v)
        npc = (valid + 511) // 512
        sc_t = None
        for p in range(npc):
            lo = p * 512
            hi = min(valid, lo + 512)
            sc_t = cop("tensor", lambda e, q0=qt, l0=lo, h0=hi: e.matmul(
                psum[:, l0:h0], q_bf[:, q0 * 128:(q0 + 1) * 128], k_bf[:, l0:h0],
                start=True, stop=True), s_t)
        wait("vector", s_t, sc_t)
        v0 = cop("vector", lambda e, q0=qt: e.tensor_tensor(
            out=psum[:, q0 * 128:(q0 + 1) * 128], in0=psum[:, q0 * 128:(q0 + 1) * 128],
            in1=cmask[:], op=ALU.add), s_v)
        wait("vector", s_v, v0)
        mx_v = cop("vector", lambda e, v0=valid: e.reduce_max(
            mx[:], psum[:, 0:v0], axis=AX.X, negate=True), s_v)
        wait("scalar", s_v, mx_v)
        ex_v = cop("scalar", lambda e, v0=valid: e.activation(
            out=probs[:, 0:v0], in_=psum[:, 0:v0], func=AF.Exp,
            bias=mx[:, 0:1], scale=1.0), s_sc)
        wait("vector", s_sc, ex_v)
        rs_v = cop("vector", lambda e, v0=valid: e.reduce_sum(
            rs[:], probs[:, 0:v0], axis=AX.X), s_v)
        wait("scalar", s_v, rs_v)
        lnv = cop("scalar", lambda e: e.activation(out=rct[:], in_=rs[:],
                                                   func=AF.Ln), s_sc)
        wait("scalar", s_sc, lnv)
        rc_v = cop("scalar", lambda e: e.activation(out=rcp[:], in_=rct[:],
                                                    func=AF.Exp, scale=-1.0), s_sc)
        wait("vector", s_sc, rc_v)
        nrm_v = cop("vector", lambda e, v0=valid: e.tensor_scalar(
            out=probs[:, 0:v0], in0=probs[:, 0:v0], scalar1=rcp[:, 0:1],
            scalar2=None, op0=ALU.mult), s_v)
        if DEBUG and qt == 1:
            wait("gpsimd", s_v, nrm_v)
            dma("gpsimd", dbg_probs1, probs[:, 0:256], s_out)
            dma("gpsimd", dbg_mx1, mx[:], s_out)
            dma("gpsimd", dbg_rs1, rs[:], s_out)
            dbg_t1 = s_out.n
            wait("vector", s_out, dbg_t1)
        wait("tensor", s_v, nrm_v)
        av_t = None
        for r0 in range(0, qt + 1, 8):
            r1 = min(qt + 1, r0 + 8)
            tr_t = None
            for j in range(r0, r1):
                tr_t = cop("tensor", lambda e, jj=j, sl=j - r0: e.transpose(
                    ps_tr[:, sl * 128:(sl + 1) * 128],
                    probs[:, jj * 128:(jj + 1) * 128], ident[:]), s_t)
            wait("vector", s_t, tr_t)
            pev = cop("vector", lambda e, a=r0, b2=r1: e.tensor_copy(
                out=probsT[:, a * 128:b2 * 128], in_=ps_tr[:, 0:(b2 - a) * 128]), s_v)
            wait("tensor", s_v, pev)
            for j in range(r0, r1):
                av_t = cop("tensor", lambda e, jj=j, q0=qt: e.matmul(
                    ps_out[:, :], v_tm[:, jj, :], probsT[:, jj * 128:(jj + 1) * 128],
                    start=(jj == 0), stop=(jj == q0)),
                    s_t if j == qt else None)
        wait("vector", s_t, av_t)
        prev_tile_v = cop("vector", lambda e, q0=qt: e.tensor_copy(
            out=attn[:, q0 * 128:(q0 + 1) * 128], in_=ps_out[:, :]), s_v)
        pump.pump(5)
        if ag_done["g"] is None and pump.done_through(KC // 2 - 1):
            trigger_ag("g", agi_g, ago_g)
        if DEBUG and qt == 1:
            wait("gpsimd", s_v, prev_tile_v)
            dma("gpsimd", dbg_probsT1, probsT[:, 0:256], s_out)
            dbg_t2 = s_out.n
            wait("vector", s_out, dbg_t2)
    attn_v = prev_tile_v
    if DEBUG:
        wait("gpsimd", s_v, attn_v)
        dma("gpsimd", dbg_attn, attn[:], s_out)

    x1wv = {}
    prev_stt = attn_v
    st2_t = 0
    for th in range(2):
        for mm in range(KC):
            slot = mm % 2
            wait_free("sync", xw_free[slot])
            xv = dma("sync", xw[:, slot, :],
                     x_d[mm * 128:(mm + 1) * 128, th * 1024:(th + 1) * 1024], s_xw[slot])
            wait("tensor", s_v, prev_stt)
            if th == 0 and mm == 0:
                wait("tensor", s_v, wo_cast)
            pp_t = None
            for p in range(2):
                pp_t = cop("tensor", lambda e, m0=mm, t0=th, pp=p: e.matmul(
                    psum[:, pp * 512:(pp + 1) * 512], wo_bf[:, m0, :],
                    attn[:, t0 * 1024 + pp * 512:t0 * 1024 + (pp + 1) * 512],
                    start=True, stop=True), s_t if p == 1 else None)
            wait("vector", s_t, pp_t)
            wait("vector", s_xw[slot], xv)
            wait_free("vector", x1h_free[slot])
            if th > 0 or mm > 0:
                wait("vector", s_sc, sq_v)
                wait("vector", s_t, st2_t)
            v0 = cop("vector", lambda e, m0=mm, sl=slot: e.scalar_tensor_tensor(
                out=x1h[:, sl, :], in0=psum[:, 0:1024], scalar=bo_sb[:, m0:m0 + 1],
                in1=xw[:, sl, :], op0=ALU.add, op1=ALU.add), s_v)
            wait("vector", s_v, v0)
            x1_v = cop("vector", lambda e, sl=slot: e.tensor_copy(
                out=xcbf[:, 0:1024], in_=x1h[:, sl, :]), s_v)
            prev_stt = x1_v
            xw_free[slot] = (s_v, x1_v)
            wait("sync", s_v, x1_v)
            x1wv[(th, mm)] = dma(
                "sync", x1_d[mm * 128:(mm + 1) * 128, th * 1024:(th + 1) * 1024],
                x1h[:, slot, :], s_x1w2[slot])
            x1h_free[slot] = (s_x1w2[slot], x1wv[(th, mm)])
            wait("scalar", s_v, x1_v)
            sq_v = cop("scalar", lambda e: e.activation(
                out=sqbf[:, 0:1024], in_=xcbf[:, 0:1024], func=AF.Square), s_sc)
            wait("tensor", s_v, x1_v)
            wait("tensor", s_sc, sq_v)
            if th == 1 and mm == 0:
                wait("tensor", s_v, st_ev)
            for p in range(2):
                cop("tensor", lambda e, m0=mm, pp=p: e.matmul(
                    psum[0:1, 2048 + pp * 512:2048 + (pp + 1) * 512], ones_c[:],
                    xcbf[:, pp * 512:(pp + 1) * 512],
                    start=(m0 == 0), stop=(m0 == KC - 1)))
            for p in range(2):
                st2_t = cop("tensor", lambda e, m0=mm, pp=p: e.matmul(
                    psum[0:1, 3072 + pp * 512:3072 + (pp + 1) * 512], ones_c[:],
                    sqbf[:, pp * 512:(pp + 1) * 512],
                    start=(m0 == 0), stop=(m0 == KC - 1)), s_t)
            pump.pump(5)
            if ag_done["g"] is None and pump.done_through(KC // 2 - 1):
                trigger_ag("g", agi_g, ago_g)
            if ag_done["u"] is None and pump.done_through(KC - 1):
                trigger_ag("u", agi_u, ago_u)
            if ag_done["d"] is None and pump.done_through(KC + BB - 1):
                trigger_ag("d", agi_dn, ago_dn)
        wait("vector", s_t, st2_t)
        v0 = cop("vector", lambda e, t0=th: e.tensor_copy(
            out=stt1[0:1, t0 * 1024:(t0 + 1) * 1024], in_=psum[0:1, 2048:3072]), s_v)
        wait("vector", s_v, v0)
        st_ev = cop("vector", lambda e, t0=th: e.tensor_copy(
            out=stt2[0:1, t0 * 1024:(t0 + 1) * 1024], in_=psum[0:1, 3072:4096]), s_v)
    wait("scalar", s_v, qkv_ev)
    ln2_done, ln2_mb = ln_stats_math_and_bcast()

    pump.flush_matrix("g")
    if ag_done["g"] is None:
        trigger_ag("g", agi_g, ago_g)

    if DEBUG:
        dma("gpsimd", dbg_wg, ago_g, s_out)

    x1_all_written = [(s_x1w2[0], s_x1w2[0].n), (s_x1w2[1], s_x1w2[1].n)]
    ag_u_done = None
    ag_d_done = None
    xn2_done = {}
    xr_free = {0: None}

    def emit_xn2_chunk(tt, kk, buf_ap, slot_key, free_map, q="scalar"):
        for sem0, v0 in x1_all_written:
            wait(q, sem0, v0)
        if free_map is not None:
            wait_free(q, free_map[slot_key])
        sem = s_xw[slot_key]
        xv = dma(q, buf_ap,
                 x1_d[kk * 128:(kk + 1) * 128, tt * 1024:(tt + 1) * 1024], sem)
        wait("vector", sem, xv)
        if kk == 0:
            wait("vector", s_mb, ln2_mb)
        v0 = cop("vector", lambda e, b=buf_ap, t0=tt: e.tensor_tensor(
            out=b, in0=b, in1=m_b[:, t0 * 1024:(t0 + 1) * 1024], op=ALU.subtract), s_v)
        wait("vector", s_v, v0)
        v = cop("vector", lambda e, b=buf_ap, k0=kk, t0=tt: e.tensor_tensor(
            out=xn2[:, k0, :], in0=b,
            in1=rstd_b[:, t0 * 1024:(t0 + 1) * 1024], op=ALU.mult), s_v)
        if free_map is not None:
            free_map[slot_key] = (s_v, v)
        return v

    xr_free = {0: None}
    xn2_chunk_done = {}
    for tt in range(2):
        for kk in range(KC):
            xn2_chunk_done[kk] = emit_xn2_chunk(tt, kk, xw[:, kk % 2, :],
                                                kk % 2, xw_free)
        xn2_done[tt] = xn2_chunk_done[KC - 1]

        if DEBUG and tt == 0:
            wait("gpsimd", s_v, xn2_done[0])
            dma("gpsimd", dbg_xn2, xn2[:, :, :], s_out)

        wait("sync", s_cc, ag_done["g"])
        silu_done = {}
        for g in range(OC):
            slot = g % 3
            r, b = g % 8, g // 8
            wait_free("sync", wgu_free[slot])
            wl = dma("sync", wgu[slot][:, :, :], ago_g[r, b], s_wgu[slot])
            wait("tensor", s_wgu[slot], wl)
            wait_free("tensor", psB_free[slot])
            g_t = None
            for kk in range(KC):
                if g == 0:
                    wait("tensor", s_v, xn2_chunk_done[kk])
                for p in range(2):
                    g_t = cop("tensor", lambda e, sl=slot, k0=kk, pp=p: e.matmul(
                        psum[:, sl * 1024 + pp * 512:sl * 1024 + (pp + 1) * 512],
                        wgu[sl][:, k0, :], xn2[:, k0, pp * 512:(pp + 1) * 512],
                        start=(k0 == 0), stop=(k0 == KC - 1)),
                        s_t if (kk == KC - 1 and p == 1) else None)
            wgu_free[slot] = (s_t, g_t)
            wait("scalar", s_t, g_t)
            sv = cop("scalar", lambda e, gg=g, sl=slot: e.activation(
                out=h_sb[:, gg, :], in_=psum[:, sl * 1024:sl * 1024 + 1024],
                func=AF.Silu), s_sc)
            psB_free[slot] = (s_sc, sv)
            if tt == 0:
                pump.pump(7)
                if ag_done["u"] is None and pump.done_through(KC - 1):
                    trigger_ag("u", agi_u, ago_u)
                if ag_done["d"] is None and pump.done_through(KC + BB - 1):
                    trigger_ag("d", agi_dn, ago_dn)
        if tt == 0:
            if ag_done["u"] is None:
                pump.flush_matrix("u")
                trigger_ag("u", agi_u, ago_u)
            if ag_done["d"] is None:
                pump.flush_matrix("d")
                trigger_ag("d", agi_dn, ago_dn)

        wait("sync", s_cc, ag_done["u"])
        for g in range(OC):
            slot = g % 3
            r, b = g % 8, g // 8
            wait_free("sync", wgu_free[slot])
            wl = dma("sync", wgu[slot][:, :, :], ago_u[r, b], s_wgu[slot])
            wait("tensor", s_wgu[slot], wl)
            wait_free("tensor", psB_free[slot])
            g_t = None
            for kk in range(KC):
                for p in range(2):
                    g_t = cop("tensor", lambda e, sl=slot, k0=kk, pp=p: e.matmul(
                        psum[:, sl * 1024 + pp * 512:sl * 1024 + (pp + 1) * 512],
                        wgu[sl][:, k0, :], xn2[:, k0, pp * 512:(pp + 1) * 512],
                        start=(k0 == 0), stop=(k0 == KC - 1)),
                        s_t if (kk == KC - 1 and p == 1) else None)
            wgu_free[slot] = (s_t, g_t)
            wait("vector", s_t, g_t)
            hv = cop("vector", lambda e, gg=g, sl=slot: e.tensor_tensor(
                out=h_sb[:, gg, :], in0=h_sb[:, gg, :],
                in1=psum[:, sl * 1024:sl * 1024 + 1024], op=ALU.mult), s_v)
            psB_free[slot] = (s_v, hv)
        h_done_v = hv

        if DEBUG and tt == 0:
            wait("gpsimd", s_v, h_done_v)
            dma("gpsimd", dbg_h, h_sb[:, :, :], s_out)
            dma("gpsimd", dbg_wd, ago_dn, s_out)
            _hv = s_out.n
            wait("scalar", s_out, _hv)

        wait("sync", s_cc, ag_done["d"])
        for mm in range(KC):
            ws3 = mm % 2
            slot = mm % 2
            ps3 = mm % 3
            wait_free("sync", wdn_free[ws3])
            wl = None
            for r in range(8):
                wl = dma("sync", wdn[ws3][:, r * 6:(r + 1) * 6, :], ago_dn[r, mm], s_wdn[ws3])
            wait_free("scalar", xw_free[slot])
            xv = dma("scalar", xw[:, slot, :],
                     x1_d[mm * 128:(mm + 1) * 128, tt * 1024:(tt + 1) * 1024], s_xw[slot])
            wait("tensor", s_wdn[ws3], wl)
            if mm == 0:
                wait("tensor", s_v, h_done_v)
            wait_free("tensor", psB_free[ps3])
            d_t = None
            for g in range(OC):
                wslot = (g % 8) * 6 + g // 8
                for p in range(2):
                    d_t = cop("tensor", lambda e, w3=ws3, sl=ps3, gg=g, ws=wslot, pp=p: e.matmul(
                        psum[:, sl * 1024 + pp * 512:sl * 1024 + (pp + 1) * 512],
                        wdn[w3][:, ws, :], h_sb[:, gg, pp * 512:(pp + 1) * 512],
                        start=(gg == 0), stop=(gg == OC - 1)),
                        s_t if (g == OC - 1 and p == 1) else None)
            wdn_free[ws3] = (s_t, d_t)
            wait("vector", s_t, d_t)
            wait("vector", s_xw[slot], xv)
            wait_free("vector", ysb_free[0])
            yv = cop("vector", lambda e, sl=slot, p3=ps3: e.tensor_tensor(
                out=y_sb[:, 0, :], in0=psum[:, p3 * 1024:p3 * 1024 + 1024],
                in1=xw[:, sl, :], op=ALU.add), s_v)
            psB_free[ps3] = (s_v, yv)
            xw_free[slot] = (s_v, yv)
            wait("gpsimd", s_v, yv)
            ov = dma("gpsimd", y_d[mm * 128:(mm + 1) * 128, tt * 1024:(tt + 1) * 1024],
                     y_sb[:, 0, :], s_out)
            ysb_free[0] = (s_out, ov)



    wait("sync", s_out, s_out.n)
    wait("gpsimd", s_out, s_out.n)

    with nc.Block() as block:
        @block.sync
        def _(e):
            for f in ops["sync"]:
                f(e)

        @block.gpsimd
        def _(e):
            for f in ops["gpsimd"]:
                f(e)

        @block.vector
        def _(e):
            for f in ops["vector"]:
                f(e)

        @block.tensor
        def _(e):
            for f in ops["tensor"]:
                f(e)

        @block.scalar
        def _(e):
            for f in ops["scalar"]:
                f(e)

    es.close()
    return nc


def _prep_inputs(L0_attn_x, Wq, bq, Wk, bk, Wv, bv, Wo, bo, lut, gidx, uidx, didx):
    x = np.asarray(L0_attn_x, np.float32)
    gidx = np.asarray(gidx, np.int32)
    uidx = np.asarray(uidx, np.int32)
    didx = np.asarray(didx, np.int32)
    in_maps = []
    for c in range(CORES):
        gblocks, ublocks, dblocks = [], [], []
        for b in range(BB):
            g = b * 8 + c
            if g >= OC:
                g = OC - 1
            gblocks.append(gidx[g * 128:(g + 1) * 128, :].T)
            ublocks.append(uidx[g * 128:(g + 1) * 128, :].T)
            dblocks.append(didx[:, g * 128:(g + 1) * 128].T)
        in_maps.append({
            "x": np.ascontiguousarray(x[c].T),
            "wq": np.ascontiguousarray(np.asarray(Wq, np.float32)),
            "wk": np.ascontiguousarray(np.asarray(Wk, np.float32)),
            "wv": np.ascontiguousarray(np.asarray(Wv, np.float32)),
            "wo": np.ascontiguousarray(np.asarray(Wo, np.float32)),
            "bq": np.ascontiguousarray(np.asarray(bq, np.float32).reshape(HD, 1)),
            "bk": np.ascontiguousarray(np.asarray(bk, np.float32).reshape(HD, 1)),
            "bv": np.ascontiguousarray(np.asarray(bv, np.float32).reshape(HD, 1)),
            "bo": np.ascontiguousarray(np.asarray(bo, np.float32).reshape(D, 1)),
            "lut": np.ascontiguousarray(np.asarray(lut, np.float32)),
            "gidx": np.ascontiguousarray(np.concatenate(gblocks, axis=1)),
            "uidx": np.ascontiguousarray(np.concatenate(ublocks, axis=1)),
            "didx": np.ascontiguousarray(np.concatenate(dblocks, axis=0)),
        })
    return in_maps


def kernel(L0_attn_x, L0_attn_Wq, L0_attn_bq, L0_attn_Wk, L0_attn_bk,
           L0_attn_Wv, L0_attn_bv, L0_attn_Wo, L0_attn_bo,
           L0_ffn_lut, L0_ffn_gate_idx, L0_ffn_up_idx, L0_ffn_down_idx):
    in_maps = _prep_inputs(L0_attn_x, L0_attn_Wq, L0_attn_bq, L0_attn_Wk,
                           L0_attn_bk, L0_attn_Wv, L0_attn_bv, L0_attn_Wo,
                           L0_attn_bo, L0_ffn_lut, L0_ffn_gate_idx,
                           L0_ffn_up_idx, L0_ffn_down_idx)
    nc = build()
    res = run_bass_kernel_spmd(nc, in_maps, core_ids=list(range(CORES)))
    out = np.stack([np.ascontiguousarray(r["y"].T) for r in res.results])
    return out.astype(np.float32)


# version 37
# speedup vs baseline: 1.1665x; 1.0118x over previous
import contextlib
import numpy as np
import concourse.bass as bass
import concourse.mybir as mybir
from concourse.bass_utils import run_bass_kernel_spmd

F32 = mybir.dt.float32
BF16 = mybir.dt.bfloat16
I32 = mybir.dt.int32
AF = mybir.ActivationFunctionType
ALU = mybir.AluOpType
AX = mybir.AxisListType

CORES = 8
D = 2048
T = 2048
HD = 128
O = 5504
KC = D // 128
OC = 43
BB = 6
NEG = -1.0e30
EPS = 1e-5
INVSQ = 1.0 / float(np.sqrt(128.0))
DEBUG = False


class SemC:
    def __init__(self, nc, es, name):
        self.h = es.enter_context(nc.semaphore(name))
        self.n = 0


def build():
    nc = bass.Bass("TRN2", target_bir_lowering=False, debug=False)

    x_d = nc.dram_tensor("x", [D, T], F32, kind="ExternalInput").ap()
    wq_d = nc.dram_tensor("wq", [D, HD], F32, kind="ExternalInput").ap()
    wk_d = nc.dram_tensor("wk", [D, HD], F32, kind="ExternalInput").ap()
    wv_d = nc.dram_tensor("wv", [D, HD], F32, kind="ExternalInput").ap()
    wo_d = nc.dram_tensor("wo", [HD, D], F32, kind="ExternalInput").ap()
    bq_d = nc.dram_tensor("bq", [HD, 1], F32, kind="ExternalInput").ap()
    bk_d = nc.dram_tensor("bk", [HD, 1], F32, kind="ExternalInput").ap()
    bv_d = nc.dram_tensor("bv", [HD, 1], F32, kind="ExternalInput").ap()
    bo_d = nc.dram_tensor("bo", [D, 1], F32, kind="ExternalInput").ap()
    lut_d = nc.dram_tensor("lut", [16], F32, kind="ExternalInput").ap()
    gidx_d = nc.dram_tensor("gidx", [D, BB * 128], I32, kind="ExternalInput").ap()
    uidx_d = nc.dram_tensor("uidx", [D, BB * 128], I32, kind="ExternalInput").ap()
    didx_d = nc.dram_tensor("didx", [BB * 128, D], I32, kind="ExternalInput").ap()
    y_d = nc.dram_tensor("y", [D, T], F32, kind="ExternalOutput").ap()
    if DEBUG:
        dbg_q = nc.dram_tensor("dbg_q", [128, T], BF16, kind="ExternalOutput").ap()
        dbg_k = nc.dram_tensor("dbg_k", [128, T], BF16, kind="ExternalOutput").ap()
        dbg_v = nc.dram_tensor("dbg_v", [128, T], BF16, kind="ExternalOutput").ap()
        dbg_vtm = nc.dram_tensor("dbg_vtm", [128, T], BF16, kind="ExternalOutput").ap()
        dbg_probs1 = nc.dram_tensor("dbg_probs1", [128, 256], BF16, kind="ExternalOutput").ap()
        dbg_probsT1 = nc.dram_tensor("dbg_probsT1", [128, 256], BF16, kind="ExternalOutput").ap()
        dbg_mx1 = nc.dram_tensor("dbg_mx1", [128, 1], F32, kind="ExternalOutput").ap()
        dbg_rs1 = nc.dram_tensor("dbg_rs1", [128, 1], F32, kind="ExternalOutput").ap()
        dbg_attn = nc.dram_tensor("dbg_attn", [128, T], BF16, kind="ExternalOutput").ap()
        dbg_rstd = nc.dram_tensor("dbg_rstd", [128, T], BF16, kind="ExternalOutput").ap()
        dbg_xn = nc.dram_tensor("dbg_xn", [16, 128, 1024], BF16, kind="ExternalOutput").ap()
        dbg_wg = nc.dram_tensor("dbg_wg", [CORES, BB, 128, KC, 128], BF16, kind="ExternalOutput").ap()
        dbg_wd = nc.dram_tensor("dbg_wd", [CORES, KC, 128, BB, 128], BF16, kind="ExternalOutput").ap()
        dbg_xn2 = nc.dram_tensor("dbg_xn2", [128, KC, 1024], BF16, kind="ExternalOutput").ap()
        dbg_h = nc.dram_tensor("dbg_h", [128, OC, 1024], BF16, kind="ExternalOutput").ap()
        dbg_qp = nc.dram_tensor("dbg_qp", [128, 1024], F32, kind="ExternalOutput").ap()


    x1_d = nc.dram_tensor("x1scr", [D, T], F32).ap()
    mean_scr = nc.dram_tensor("mean_scr", [1, T], F32).ap()
    dumi = nc.dram_tensor("dumi", [128, 16], F32).ap()
    dumo = nc.dram_tensor("dumo", [8, 128, 16], F32, addr_space="Shared").ap()
    agi_g = nc.dram_tensor("agi_g", [BB, 128, KC, 128], BF16).ap()
    agi_u = nc.dram_tensor("agi_u", [BB, 128, KC, 128], BF16).ap()
    agi_dn = nc.dram_tensor("agi_dn", [KC, 128, BB, 128], BF16).ap()
    ago_g = nc.dram_tensor("ago_g", [CORES, BB, 128, KC, 128], BF16,
                           addr_space="Shared").ap()
    ago_u = nc.dram_tensor("ago_u", [CORES, BB, 128, KC, 128], BF16,
                           addr_space="Shared").ap()
    ago_dn = nc.dram_tensor("ago_dn", [CORES, KC, 128, BB, 128], BF16,
                            addr_space="Shared").ap()

    es = contextlib.ExitStack()
    sb = lambda name, shape, dt: es.enter_context(nc.sbuf_tensor(name, shape, dt))

    lutb = sb("lutb", [128, 16], F32)
    iota_p = sb("iota_p", [128, 1], F32)
    ones_c = sb("ones_c", [128, 1], BF16)
    ones_r = sb("ones_r", [1, 128], BF16)
    rstd_b = sb("rstd_b", [128, T], BF16)
    m_b = sb("m_b", [128, T], F32)
    gi = sb("gi", [128, 2048], I32)
    gibf = sb("gibf", [128, 2048], BF16)
    gu = sb("gu", [128, 2048], BF16)
    gw = sb("gw", [128, 2, 2048], BF16)
    xw = sb("xw", [128, 2, 1024], F32)
    y_sb = sb("y_sb", [128, 1, 1024], F32)
    bqkv = sb("bqkv", [128, 3], F32)
    bo_sb = sb("bo_sb", [128, KC], F32)

    base = nc._sbuf_addr_for_side("left")

    class Region:
        def __init__(self, off):
            self.off = off

        def take(self, name, shape, dt):
            nbytes = int(np.prod(shape[1:])) * mybir.dt.size(dt)
            t = nc.alloc_sbuf_tensor_at(name, shape, dt, offset=self.off)
            self.off += (nbytes + 63) // 64 * 64
            assert self.off <= nc.SBUF_PARTITION_SIZE_BYTES - 1024, (name, self.off)
            return t.ap()

    A = Region(base)
    iota_f = A.take("iota_f", [128, 128], F32)
    ident = A.take("ident", [128, 128], BF16)
    cmask = A.take("cmask", [128, 128], F32)
    stt1 = A.take("stt1", [1, T], F32)
    stt2 = A.take("stt2", [1, T], F32)
    stt3 = A.take("stt3", [1, T], F32)
    st_bf = A.take("st_bf", [1, 2 * T], BF16)
    wqkv_f = A.take("wqkv_f", [128, KC, 128], F32)
    wq_bf = A.take("wq_bf", [128, KC, 128], BF16)
    wk_bf = A.take("wk_bf", [128, KC, 128], BF16)
    wv_bf = A.take("wv_bf", [128, KC, 128], BF16)
    wo_bf = A.take("wo_bf", [128, KC, 128], BF16)
    q_bf = A.take("q_bf", [128, T], BF16)
    k_bf = A.take("k_bf", [128, T], BF16)
    v_bf = A.take("v_bf", [128, T], BF16)
    v_tm = A.take("v_tm", [128, KC, 128], BF16)
    probs = A.take("probs", [128, T], BF16)
    probsT = A.take("probsT", [128, T], BF16)
    attn = A.take("attn", [128, T], BF16)
    xnc = A.take("xnc", [128, 2, 1024], BF16)
    xcbf = A.take("xcbf", [128, 4096], BF16)
    sqbf = A.take("sqbf", [128, 4096], BF16)
    x1h = A.take("x1h", [128, 2, 1024], F32)
    xa = A.take("xa", [128, 2, 2048], F32)
    mx = A.take("mx", [128, 1], F32)
    rs = A.take("rs", [128, 1], F32)
    rcp = A.take("rcp", [128, 1], F32)
    rct = A.take("rct", [128, 1], F32)

    B = Region(base)
    h_sb = B.take("h_sb", [128, OC, 1024], BF16)
    xn2 = B.take("xn2", [128, KC, 1024], BF16)
    wgu = [B.take(f"wgu{i}", [128, KC, 128], BF16) for i in range(3)]
    wdn = [B.take(f"wdn{i}", [128, 48, 128], BF16) for i in range(2)]

    psum = es.enter_context(nc.psum_tensor("psum", [128, 4096], F32))
    ps_tr = psum[:, 3072:3584].bitcast(BF16)
    ps_out = psum[:, 3584:3712]

    s_ld = SemC(nc, es, "s_ld")
    s_xa = [SemC(nc, es, f"s_xa{i}") for i in range(2)]
    s_xw = [SemC(nc, es, f"s_xw{i}") for i in range(2)]
    s_dum = SemC(nc, es, "s_dum")
    s_gi2 = [SemC(nc, es, f"s_gi{i}") for i in range(2)]
    s_st2 = [SemC(nc, es, f"s_st{i}") for i in range(2)]
    s_st3 = [SemC(nc, es, f"s_su{i}") for i in range(2)]
    s_wgu = [SemC(nc, es, f"s_wgu{i}") for i in range(3)]
    s_wdn = [SemC(nc, es, f"s_wdn{i}") for i in range(3)]
    s_out = SemC(nc, es, "s_out")
    s_x1w2 = [SemC(nc, es, f"s_x1w{i}") for i in range(2)]
    s_mb = SemC(nc, es, "s_mb")
    s_cc = SemC(nc, es, "s_cc")
    s_v = SemC(nc, es, "s_v")
    s_t = SemC(nc, es, "s_t")
    s_sc = SemC(nc, es, "s_sc")
    s_g = SemC(nc, es, "s_g")
    s_bar = SemC(nc, es, "s_bar")

    ops = {"sync": [], "gpsimd": [], "vector": [], "tensor": [], "scalar": []}

    def op(eng, fn):
        ops[eng].append(fn)

    def wait(eng, s, val):
        if val > 0:
            ops[eng].append(lambda e, h=s.h, v=val: e.wait_ge(h, v))

    def dma(eng, out, in_, s):
        s.n += 16
        ops[eng].append(lambda e, o=out, i=in_, h=s.h: e.dma_start(out=o, in_=i).then_inc(h, 16))
        return s.n

    def cop(eng, fn, s=None):
        if s is None:
            ops[eng].append(lambda e, f=fn: f(e))
            return None
        s.n += 1
        ops[eng].append(lambda e, f=fn, h=s.h: f(e).then_inc(h, 1))
        return s.n

    xw_free = {0: None, 1: None}
    x1h_free = {0: None, 1: None}
    ysb_free = {0: None, 1: None}
    psB_free = {0: None, 1: None, 2: None}
    wgu_free = {0: None, 1: None, 2: None}
    wdn_free = {0: None, 1: None}

    def wait_free(eng, entry):
        if entry is not None:
            wait(eng, entry[0], entry[1])

    dma("sync", lutb[:], lut_d.partition_broadcast(128), s_ld)
    lut_loaded = s_ld.n
    dma("sync", bqkv[:, 0:1], bq_d, s_ld)
    dma("sync", bqkv[:, 1:2], bk_d, s_ld)
    dma("sync", bqkv[:, 2:3], bv_d, s_ld)
    for mm in range(KC):
        dma("sync", bo_sb[:, mm:mm + 1], bo_d[mm * 128:(mm + 1) * 128, :], s_ld)
    small_ld = s_ld.n
    cop("gpsimd", lambda e: e.dma_start(out=dumi, in_=lut_d.partition_broadcast(128)).then_inc(s_dum.h, 16))
    s_dum.n += 16
    wait("gpsimd", s_dum, s_dum.n)
    cop("gpsimd", lambda e: e.collective_compute(
        "AllGather", ALU.bypass, replica_groups=[list(range(CORES))],
        ins=[dumi.opt()], outs=[dumo.opt()]), s_cc)
    cop("gpsimd", lambda e: e.iota(iota_p[:], pattern=[[0, 1]], channel_multiplier=1,
                                   allow_small_or_imprecise_dtypes=True))
    cop("gpsimd", lambda e: e.iota(iota_f[:], pattern=[[1, 128]], channel_multiplier=0,
                                   allow_small_or_imprecise_dtypes=True), s_g)
    wait("vector", s_g, s_g.n)
    cop("vector", lambda e: e.tensor_scalar(out=ident[:], in0=iota_f[:],
                                            scalar1=iota_p[:, 0:1], scalar2=None,
                                            op0=ALU.is_equal))
    cop("vector", lambda e: e.tensor_scalar(out=cmask[:], in0=iota_f[:],
                                            scalar1=iota_p[:, 0:1], scalar2=NEG,
                                            op0=ALU.is_gt, op1=ALU.mult))
    cop("vector", lambda e: e.memset(ones_c[:], 1.0))
    cop("vector", lambda e: e.memset(ones_r[:], 1.0), s_v)

    wcast = []
    for w_d, w_bf, along_rows in ((wq_d, wq_bf, True), (wk_d, wk_bf, True),
                                  (wv_d, wv_bf, True), (wo_d, wo_bf, False)):
        if wcast:
            wait("sync", s_v, wcast[-1])
        for kk in range(KC):
            src = (w_d[kk * 128:(kk + 1) * 128, :] if along_rows
                   else w_d[:, kk * 128:(kk + 1) * 128])
            v = dma("sync", wqkv_f[:, kk, :], src, s_ld)
        wait("vector", s_ld, v)
        wcast.append(cop("vector", lambda e, b=w_bf: e.tensor_copy(
            out=b[:, :, :], in_=wqkv_f[:, :, :]), s_v))
    wqkv_cast, wo_cast = wcast[2], wcast[3]

    gather_tiles = ([("g", kk) for kk in range(0, KC, 2)]
                    + [("u", kk) for kk in range(0, KC, 2)]
                    + [("d", b) for b in range(BB)])
    g_seq = {t: i for i, t in enumerate(gather_tiles)}
    stage_done_v = {}
    gath_done_v = {}
    cast_done_v = {}
    pump = None

    def gather_idx_load(tile):
        seq = g_seq[tile]
        if seq >= 1:
            wait("gpsimd", s_v, cast_done_v[gather_tiles[seq - 1]])
        if tile[0] in ("g", "u"):
            idx_d = gidx_d if tile[0] == "g" else uidx_d
            kk = tile[1]
            v0 = dma("gpsimd", gi[:, 0:768],
                     idx_d[kk * 128:(kk + 1) * 128, :], s_gi2[0])
            v1 = dma("gpsimd", gi[:, 768:1536],
                     idx_d[(kk + 1) * 128:(kk + 2) * 128, :], s_gi2[1])
            return [(s_gi2[0], v0), (s_gi2[1], v1)]
        _, b = tile
        v0 = dma("gpsimd", gi[:, 0:2048],
                 didx_d[b * 128:(b + 1) * 128, :], s_gi2[0])
        return [(s_gi2[0], v0)]

    def gather_stage(tile):
        slot = g_seq[tile] % 2
        wait("gpsimd", s_v, gath_done_v[tile])
        if tile[0] in ("g", "u"):
            kk = tile[1]
            agi = agi_g if tile[0] == "g" else agi_u
            done = []
            for h in range(2):
                sem = (s_st2 if h == 0 else s_st3)[slot]
                dst = agi[:, :, kk + h, :].transpose([1, 0, 2])
                src = gw[:, slot, h * 768:(h + 1) * 768].rearrange(
                    "p (b j) -> p b j", b=BB)
                done.append((sem, dma("gpsimd", dst, src, sem)))
        else:
            _, b = tile
            done = []
            for hf in range(2):
                sem = (s_st2 if hf == 0 else s_st3)[slot]
                dst = agi_dn[hf * 8:(hf + 1) * 8, :, b, :].transpose([1, 0, 2])
                src = gw[:, slot, hf * 1024:(hf + 1) * 1024].rearrange(
                    "p (m j) -> p m j", m=8)
                done.append((sem, dma("gpsimd", dst, src, sem)))
        stage_done_v[tile] = done

    class GPump:
        def __init__(self, tiles):
            self.tiles = list(tiles)
            self.ti = 0
            self.q = []
            self.cur = None

        def _refill(self):
            if self.ti >= len(self.tiles):
                return False
            tile = self.tiles[self.ti]
            self.ti += 1
            self.cur = tile
            seq = g_seq[tile]
            slot = seq % 2
            F = 2048 if tile[0] == "d" else 1536
            iv = gather_idx_load(tile)
            ops_ = []
            def first(e=None, f=F, t=tile, iv=iv, seq=seq):
                for sem0, v0 in iv:
                    wait("vector", sem0, v0)
                if seq >= 2:
                    for sem0, v0 in stage_done_v[gather_tiles[seq - 2]]:
                        wait("vector", sem0, v0)
                v0 = cop("vector", lambda e2, f=f: e2.tensor_copy(
                    out=gibf[:, 0:f], in_=gi[:, 0:f]), s_v)
                cast_done_v[t] = v0
                wait("vector", s_v, v0)
            ops_.append(first)

            def sv_op(fn):
                wait("vector", s_v, s_v.n)
                cop("vector", fn, s_v)

            def z0(sl=slot, f=F):
                sv_op(lambda e2: e2.tensor_scalar(
                    out=gw[:, sl, 0:f], in0=gibf[:, 0:f], scalar1=0.0,
                    scalar2=lutb[:, 0:1], op0=ALU.is_equal, op1=ALU.mult))
            ops_.append(z0)
            for j in range(1, 16):
                def uo(jj=j, f=F):
                    sv_op(lambda e2: e2.tensor_scalar(
                        out=gu[:, 0:f], in0=gibf[:, 0:f], scalar1=float(jj),
                        scalar2=lutb[:, jj:jj + 1], op0=ALU.is_equal, op1=ALU.mult))
                ops_.append(uo)
                if j < 15:
                    def ao(sl=slot, f=F):
                        sv_op(lambda e2: e2.tensor_tensor(
                            out=gw[:, sl, 0:f], in0=gw[:, sl, 0:f], in1=gu[:, 0:f],
                            op=ALU.add))
                    ops_.append(ao)
                else:
                    def alast(sl=slot, f=F, t=tile):
                        wait("vector", s_v, s_v.n)
                        cop("vector", lambda e2: e2.tensor_tensor(
                            out=gw[:, sl, 0:f], in0=gw[:, sl, 0:f], in1=gu[:, 0:f],
                            op=ALU.add), s_v)
                        gath_done_v[t] = s_v.n
                        gather_stage(t)
                    ops_.append(alast)
            self.q = ops_
            return True

        def pump(self, n):
            k = 0
            while k < n:
                if not self.q:
                    if not self._refill():
                        return
                self.q.pop(0)()
                k += 1

        def flush_matrix(self, mat):
            while any(t[0] == mat for t in self.tiles[self.ti:]) or (
                    self.cur is not None and self.cur[0] == mat and self.q):
                if not self.q and not self._refill():
                    return
                while self.q:
                    self.q.pop(0)()

        def done_through(self, idx):
            return self.ti > idx + 1 or (self.ti == idx + 1 and not self.q)

    def serial_v(fn):
        v0 = cop("vector", fn, s_v)
        wait("vector", s_v, v0)
        return v0

    def ln_stats_math_and_bcast():
        wait("vector", s_v, s_v.n)
        mn = serial_v(lambda e: e.tensor_scalar(
            out=stt1[:], in0=stt1[:], scalar1=1.0 / D, scalar2=None, op0=ALU.mult))
        wait("scalar", s_v, mn)
        serial_v(lambda e: e.tensor_scalar(
            out=stt2[:], in0=stt2[:], scalar1=1.0 / D, scalar2=None, op0=ALU.mult))
        serial_v(lambda e: e.tensor_tensor(
            out=stt3[:], in0=stt1[:], in1=stt1[:], op=ALU.mult))
        serial_v(lambda e: e.tensor_tensor(
            out=stt2[:], in0=stt2[:], in1=stt3[:], op=ALU.subtract))
        v = serial_v(lambda e: e.tensor_scalar(
            out=stt2[:], in0=stt2[:], scalar1=EPS, scalar2=None, op0=ALU.add))
        wait("scalar", s_v, v)
        v = cop("scalar", lambda e: e.activation(out=stt3[:], in_=stt2[:],
                                                 func=AF.Ln), s_sc)
        wait("scalar", s_sc, v)
        v = cop("scalar", lambda e: e.activation(out=st_bf[:, 0:T], in_=stt3[:],
                                                 func=AF.Exp, scale=-0.5), s_sc)
        wait("vector", s_sc, v)
        mv = dma("scalar", mean_scr, stt1[0:1, :], s_mb)
        wait("scalar", s_mb, mv)
        mv = dma("scalar", m_b[:], mean_scr.partition_broadcast(128), s_mb)
        wait("tensor", s_sc, v)
        tv = None
        for p in range(4):
            tv = cop("tensor", lambda e, pp=p: e.matmul(
                psum[:, pp * 512:(pp + 1) * 512], ones_r[:],
                st_bf[:, pp * 512:(pp + 1) * 512],
                start=True, stop=True), s_t)
        wait("vector", s_t, tv)
        out_v = cop("vector", lambda e: e.tensor_copy(
            out=rstd_b[:], in_=psum[:, 0:T]), s_v)
        wait("tensor", s_v, out_v)
        return out_v, mv

    pump = GPump(gather_tiles)

    xa_free = {0: None, 1: None}
    xc_free = {0: None, 1: None}
    cast_done = {}
    sq_done = {}
    stat_done = {}
    for kk in range(KC):
        sl = kk % 2
        wait_free("sync", xa_free[sl])
        xv = dma("sync", xa[:, sl, :], x_d[kk * 128:(kk + 1) * 128, :], s_xa[sl])
        wait("vector", s_xa[sl], xv)
        if kk >= 2:
            wait("vector", s_t, stat_done[kk - 2])
        cast_done[kk] = cop("vector", lambda e, q0=sl: e.tensor_copy(
            out=xcbf[:, q0 * 2048:(q0 + 1) * 2048], in_=xa[:, q0, :]), s_v)
        xa_free[sl] = (s_v, cast_done[kk])
        wait("scalar", s_v, cast_done[kk])
        sq_done[kk] = cop("scalar", lambda e, q0=sl: e.activation(
            out=sqbf[:, q0 * 2048:(q0 + 1) * 2048],
            in_=xcbf[:, q0 * 2048:(q0 + 1) * 2048], func=AF.Square), s_sc)
        wait("tensor", s_v, cast_done[kk])
        wait("tensor", s_sc, sq_done[kk])
        for p in range(4):
            cop("tensor", lambda e, q0=sl, pp=p, k0=kk: e.matmul(
                psum[0:1, pp * 512:(pp + 1) * 512], ones_c[:],
                xcbf[:, q0 * 2048 + pp * 512:q0 * 2048 + (pp + 1) * 512],
                start=(k0 == 0), stop=(k0 == KC - 1)))
        for p in range(4):
            stat_done[kk] = cop("tensor", lambda e, q0=sl, pp=p, k0=kk: e.matmul(
                psum[0:1, 2048 + pp * 512:2048 + (pp + 1) * 512], ones_c[:],
                sqbf[:, q0 * 2048 + pp * 512:q0 * 2048 + (pp + 1) * 512],
                start=(k0 == 0), stop=(k0 == KC - 1)), s_t)
        pump.pump(10)
    sqst_t = stat_done[KC - 1]
    wait("vector", s_t, sqst_t)
    v = cop("vector", lambda e: e.tensor_copy(out=stt1[:], in_=psum[0:1, 0:2048]), s_v)
    wait("vector", s_v, v)
    v = cop("vector", lambda e: e.tensor_copy(out=stt2[:], in_=psum[0:1, 2048:4096]), s_v)
    wait("tensor", s_v, v)
    ln1_done, ln1_mb = ln_stats_math_and_bcast()
    ag_done = {"g": None, "u": None, "d": None}

    def trigger_ag(key, agi, ago):
        wait("gpsimd", s_st2[0], s_st2[0].n)
        wait("gpsimd", s_st2[1], s_st2[1].n)
        wait("gpsimd", s_st3[0], s_st3[0].n)
        wait("gpsimd", s_st3[1], s_st3[1].n)
        cop("gpsimd", lambda e, i0=agi, o0=ago: e.collective_compute(
            "AllGather", ALU.bypass, replica_groups=[list(range(CORES))],
            ins=[i0.opt()], outs=[o0.opt()]), s_cc)
        ag_done[key] = s_cc.n

    if DEBUG:
        wait("gpsimd", s_v, ln1_done)
        dma("gpsimd", dbg_rstd, rstd_b[:], s_out)
        pass

    xnc_free = {0: None, 1: None}
    qkv_ev = 0
    for th in range(2):
        for kk in range(KC):
            slot = kk % 2
            wait_free("sync", xw_free[slot])
            xv = dma("sync", xw[:, slot, :],
                     x_d[kk * 128:(kk + 1) * 128, th * 1024:(th + 1) * 1024], s_xw[slot])
            wait("vector", s_xw[slot], xv)
            if th == 0 and kk == 0:
                wait("vector", s_mb, ln1_mb)
            wait_free("vector", xnc_free[slot])
            v0 = cop("vector", lambda e, t0=th, sl=slot: e.tensor_tensor(
                out=xw[:, sl, :], in0=xw[:, sl, :],
                in1=m_b[:, t0 * 1024:(t0 + 1) * 1024], op=ALU.subtract), s_v)
            wait("vector", s_v, v0)
            xn_v = cop("vector", lambda e, t0=th, sl=slot: e.tensor_tensor(
                out=xnc[:, sl, :], in0=xw[:, sl, :],
                in1=rstd_b[:, t0 * 1024:(t0 + 1) * 1024], op=ALU.mult), s_v)
            xw_free[slot] = (s_v, xn_v)
            if DEBUG and th == 0:
                wait("gpsimd", s_v, xn_v)
                dma("gpsimd", dbg_xn[kk], xnc[:, kk % 2, :], s_out)
                _dv = s_out.n
                wait("vector", s_out, _dv)
            wait("tensor", s_v, xn_v)
            if th == 0 and kk == 0:
                wait("tensor", s_v, wqkv_cast)
                wait("tensor", s_v, ln1_done)
            if th == 1 and kk == 0:
                wait("tensor", s_v, qkv_ev)
            qk_t = None
            for wi, wbf in enumerate((wq_bf, wk_bf, wv_bf)):
                for p in range(2):
                    qk_t = cop("tensor", lambda e, w=wbf, k0=kk, wi0=wi, pp=p, sl=slot: e.matmul(
                        psum[:, wi0 * 1024 + pp * 512: wi0 * 1024 + (pp + 1) * 512],
                        w[:, k0, :],
                        xnc[:, sl, pp * 512:(pp + 1) * 512],
                        start=(k0 == 0), stop=(k0 == KC - 1)),
                        s_t if (wi == 2 and p == 1) else None)
            xnc_free[slot] = (s_t, qk_t)
            pump.pump(3)
        wait("vector", s_t, qk_t)
        if DEBUG and th == 0:
            cop("vector", lambda e: e.tensor_copy(out=x1h[:, 0, :], in_=psum[:, 0:1024]), s_v)
            wait("gpsimd", s_v, s_v.n)
            dma("gpsimd", dbg_qp, x1h[:, 0, :], s_out)
            wait("vector", s_out, s_out.n)
        cop("vector", lambda e, t0=th: e.tensor_scalar(
            out=q_bf[:, t0 * 1024:(t0 + 1) * 1024], in0=psum[:, 0:1024],
            scalar1=bqkv[:, 0:1], scalar2=INVSQ, op0=ALU.add, op1=ALU.mult))
        cop("vector", lambda e, t0=th: e.tensor_scalar(
            out=k_bf[:, t0 * 1024:(t0 + 1) * 1024], in0=psum[:, 1024:2048],
            scalar1=bqkv[:, 1:2], scalar2=None, op0=ALU.add))
        qkv_ev = cop("vector", lambda e, t0=th: e.tensor_scalar(
            out=v_bf[:, t0 * 1024:(t0 + 1) * 1024], in0=psum[:, 2048:3072],
            scalar1=bqkv[:, 2:3], scalar2=None, op0=ALU.add), s_v)

    wait("tensor", s_v, qkv_ev)
    ev = 0
    for r0 in range(0, KC, 8):
        tv = None
        for j in range(r0, r0 + 8):
            tv = cop("tensor", lambda e, jj=j, sl=j - r0: e.transpose(
                ps_tr[:, sl * 128:(sl + 1) * 128],
                v_bf[:, jj * 128:(jj + 1) * 128], ident[:]), s_t)
        wait("vector", s_t, tv)
        ev = cop("vector", lambda e, a=r0: e.tensor_copy(
            out=v_tm[:, a:a + 8, :].rearrange("p a b -> p (a b)"),
            in_=ps_tr[:, 0:1024]), s_v)
        if r0 == 0:
            wait("tensor", s_v, ev)

    if DEBUG:
        wait("gpsimd", s_v, qkv_ev)
        dma("gpsimd", dbg_q, q_bf[:], s_out)
        dma("gpsimd", dbg_k, k_bf[:], s_out)
        dma("gpsimd", dbg_v, v_bf[:], s_out)
        wait("gpsimd", s_v, ev)
        dma("gpsimd", dbg_vtm, v_tm.rearrange("p a b -> p (a b)"), s_out)

    prev_tile_v = ev
    for qt in range(KC):
        valid = (qt + 1) * 128
        wait("tensor", s_v, prev_tile_v)
        npc = (valid + 511) // 512
        sc_t = None
        for p in range(npc):
            lo = p * 512
            hi = min(valid, lo + 512)
            sc_t = cop("tensor", lambda e, q0=qt, l0=lo, h0=hi: e.matmul(
                psum[:, l0:h0], q_bf[:, q0 * 128:(q0 + 1) * 128], k_bf[:, l0:h0],
                start=True, stop=True), s_t)
        wait("vector", s_t, sc_t)
        v0 = cop("vector", lambda e, q0=qt: e.tensor_tensor(
            out=psum[:, q0 * 128:(q0 + 1) * 128], in0=psum[:, q0 * 128:(q0 + 1) * 128],
            in1=cmask[:], op=ALU.add), s_v)
        wait("vector", s_v, v0)
        mx_v = cop("vector", lambda e, v0=valid: e.reduce_max(
            mx[:], psum[:, 0:v0], axis=AX.X, negate=True), s_v)
        wait("scalar", s_v, mx_v)
        ex_v = cop("scalar", lambda e, v0=valid: e.activation(
            out=probs[:, 0:v0], in_=psum[:, 0:v0], func=AF.Exp,
            bias=mx[:, 0:1], scale=1.0), s_sc)
        wait("vector", s_sc, ex_v)
        rs_v = cop("vector", lambda e, v0=valid: e.reduce_sum(
            rs[:], probs[:, 0:v0], axis=AX.X), s_v)
        wait("scalar", s_v, rs_v)
        lnv = cop("scalar", lambda e: e.activation(out=rct[:], in_=rs[:],
                                                   func=AF.Ln), s_sc)
        wait("scalar", s_sc, lnv)
        rc_v = cop("scalar", lambda e: e.activation(out=rcp[:], in_=rct[:],
                                                    func=AF.Exp, scale=-1.0), s_sc)
        wait("vector", s_sc, rc_v)
        nrm_v = cop("vector", lambda e, v0=valid: e.tensor_scalar(
            out=probs[:, 0:v0], in0=probs[:, 0:v0], scalar1=rcp[:, 0:1],
            scalar2=None, op0=ALU.mult), s_v)
        if DEBUG and qt == 1:
            wait("gpsimd", s_v, nrm_v)
            dma("gpsimd", dbg_probs1, probs[:, 0:256], s_out)
            dma("gpsimd", dbg_mx1, mx[:], s_out)
            dma("gpsimd", dbg_rs1, rs[:], s_out)
            dbg_t1 = s_out.n
            wait("vector", s_out, dbg_t1)
        wait("tensor", s_v, nrm_v)
        av_t = None
        for r0 in range(0, qt + 1, 8):
            r1 = min(qt + 1, r0 + 8)
            tr_t = None
            for j in range(r0, r1):
                tr_t = cop("tensor", lambda e, jj=j, sl=j - r0: e.transpose(
                    ps_tr[:, sl * 128:(sl + 1) * 128],
                    probs[:, jj * 128:(jj + 1) * 128], ident[:]), s_t)
            wait("vector", s_t, tr_t)
            pev = cop("vector", lambda e, a=r0, b2=r1: e.tensor_copy(
                out=probsT[:, a * 128:b2 * 128], in_=ps_tr[:, 0:(b2 - a) * 128]), s_v)
            wait("tensor", s_v, pev)
            for j in range(r0, r1):
                av_t = cop("tensor", lambda e, jj=j, q0=qt: e.matmul(
                    ps_out[:, :], v_tm[:, jj, :], probsT[:, jj * 128:(jj + 1) * 128],
                    start=(jj == 0), stop=(jj == q0)),
                    s_t if j == qt else None)
        wait("vector", s_t, av_t)
        prev_tile_v = cop("vector", lambda e, q0=qt: e.tensor_copy(
            out=attn[:, q0 * 128:(q0 + 1) * 128], in_=ps_out[:, :]), s_v)
        pump.pump(5)
        if ag_done["g"] is None and pump.done_through(KC // 2 - 1):
            trigger_ag("g", agi_g, ago_g)
        if DEBUG and qt == 1:
            wait("gpsimd", s_v, prev_tile_v)
            dma("gpsimd", dbg_probsT1, probsT[:, 0:256], s_out)
            dbg_t2 = s_out.n
            wait("vector", s_out, dbg_t2)
    attn_v = prev_tile_v
    if DEBUG:
        wait("gpsimd", s_v, attn_v)
        dma("gpsimd", dbg_attn, attn[:], s_out)

    x1wv = {}
    prev_stt = attn_v
    st2_t = 0
    for th in range(2):
        for mm in range(KC):
            slot = mm % 2
            wait_free("sync", xw_free[slot])
            xv = dma("sync", xw[:, slot, :],
                     x_d[mm * 128:(mm + 1) * 128, th * 1024:(th + 1) * 1024], s_xw[slot])
            wait("tensor", s_v, prev_stt)
            if th == 0 and mm == 0:
                wait("tensor", s_v, wo_cast)
            pp_t = None
            for p in range(2):
                pp_t = cop("tensor", lambda e, m0=mm, t0=th, pp=p: e.matmul(
                    psum[:, pp * 512:(pp + 1) * 512], wo_bf[:, m0, :],
                    attn[:, t0 * 1024 + pp * 512:t0 * 1024 + (pp + 1) * 512],
                    start=True, stop=True), s_t if p == 1 else None)
            wait("vector", s_t, pp_t)
            wait("vector", s_xw[slot], xv)
            wait_free("vector", x1h_free[slot])
            if th > 0 or mm > 0:
                wait("vector", s_sc, sq_v)
                wait("vector", s_t, st2_t)
            v0 = cop("vector", lambda e, m0=mm, sl=slot: e.scalar_tensor_tensor(
                out=x1h[:, sl, :], in0=psum[:, 0:1024], scalar=bo_sb[:, m0:m0 + 1],
                in1=xw[:, sl, :], op0=ALU.add, op1=ALU.add), s_v)
            wait("vector", s_v, v0)
            x1_v = cop("vector", lambda e, sl=slot: e.tensor_copy(
                out=xcbf[:, 0:1024], in_=x1h[:, sl, :]), s_v)
            prev_stt = x1_v
            xw_free[slot] = (s_v, x1_v)
            wait("sync", s_v, x1_v)
            x1wv[(th, mm)] = dma(
                "sync", x1_d[mm * 128:(mm + 1) * 128, th * 1024:(th + 1) * 1024],
                x1h[:, slot, :], s_x1w2[slot])
            x1h_free[slot] = (s_x1w2[slot], x1wv[(th, mm)])
            wait("scalar", s_v, x1_v)
            sq_v = cop("scalar", lambda e: e.activation(
                out=sqbf[:, 0:1024], in_=xcbf[:, 0:1024], func=AF.Square), s_sc)
            wait("tensor", s_v, x1_v)
            wait("tensor", s_sc, sq_v)
            if th == 1 and mm == 0:
                wait("tensor", s_v, st_ev)
            for p in range(2):
                cop("tensor", lambda e, m0=mm, pp=p: e.matmul(
                    psum[0:1, 2048 + pp * 512:2048 + (pp + 1) * 512], ones_c[:],
                    xcbf[:, pp * 512:(pp + 1) * 512],
                    start=(m0 == 0), stop=(m0 == KC - 1)))
            for p in range(2):
                st2_t = cop("tensor", lambda e, m0=mm, pp=p: e.matmul(
                    psum[0:1, 3072 + pp * 512:3072 + (pp + 1) * 512], ones_c[:],
                    sqbf[:, pp * 512:(pp + 1) * 512],
                    start=(m0 == 0), stop=(m0 == KC - 1)), s_t)
            pump.pump(5)
            if ag_done["g"] is None and pump.done_through(KC // 2 - 1):
                trigger_ag("g", agi_g, ago_g)
            if ag_done["u"] is None and pump.done_through(KC - 1):
                trigger_ag("u", agi_u, ago_u)
            if ag_done["d"] is None and pump.done_through(KC + BB - 1):
                trigger_ag("d", agi_dn, ago_dn)
        wait("vector", s_t, st2_t)
        v0 = cop("vector", lambda e, t0=th: e.tensor_copy(
            out=stt1[0:1, t0 * 1024:(t0 + 1) * 1024], in_=psum[0:1, 2048:3072]), s_v)
        wait("vector", s_v, v0)
        st_ev = cop("vector", lambda e, t0=th: e.tensor_copy(
            out=stt2[0:1, t0 * 1024:(t0 + 1) * 1024], in_=psum[0:1, 3072:4096]), s_v)
    wait("scalar", s_v, qkv_ev)
    ln2_done, ln2_mb = ln_stats_math_and_bcast()

    pump.flush_matrix("g")
    if ag_done["g"] is None:
        trigger_ag("g", agi_g, ago_g)

    if DEBUG:
        dma("gpsimd", dbg_wg, ago_g, s_out)

    x1_all_written = [(s_x1w2[0], s_x1w2[0].n), (s_x1w2[1], s_x1w2[1].n)]
    ag_u_done = None
    ag_d_done = None
    xn2_done = {}
    xr_free = {0: None}

    def emit_xn2_chunk(tt, kk, buf_ap, slot_key, free_map, q="scalar", sems=None):
        for sem0, v0 in x1_all_written:
            wait(q, sem0, v0)
        if free_map is not None:
            wait_free(q, free_map[slot_key])
        sem = (sems or s_xw)[slot_key]
        xv = dma(q, buf_ap,
                 x1_d[kk * 128:(kk + 1) * 128, tt * 1024:(tt + 1) * 1024], sem)
        wait("vector", sem, xv)
        if kk == 0:
            wait("vector", s_mb, ln2_mb)
        v0 = cop("vector", lambda e, b=buf_ap, t0=tt: e.tensor_tensor(
            out=b, in0=b, in1=m_b[:, t0 * 1024:(t0 + 1) * 1024], op=ALU.subtract), s_v)
        wait("vector", s_v, v0)
        v = cop("vector", lambda e, b=buf_ap, k0=kk, t0=tt: e.tensor_tensor(
            out=xn2[:, k0, :], in0=b,
            in1=rstd_b[:, t0 * 1024:(t0 + 1) * 1024], op=ALU.mult), s_v)
        if free_map is not None:
            free_map[slot_key] = (s_v, v)
        return v

    xr_free = {0: None}
    xr2_free = {0: None, 1: None}
    xn2_chunk_done = {}
    for tt in range(2):
        if tt == 0:
            for kk in range(KC):
                xn2_chunk_done[kk] = emit_xn2_chunk(tt, kk, xw[:, kk % 2, :],
                                                    kk % 2, xw_free)
            xn2_done[tt] = xn2_chunk_done[KC - 1]

        if DEBUG and tt == 0:
            wait("gpsimd", s_v, xn2_done[0])
            dma("gpsimd", dbg_xn2, xn2[:, :, :], s_out)

        wait("sync", s_cc, ag_done["g"])
        silu_done = {}
        for g in range(OC):
            slot = g % 3
            r, b = g % 8, g // 8
            wait_free("sync", wgu_free[slot])
            wl = dma("sync", wgu[slot][:, :, :], ago_g[r, b], s_wgu[slot])
            wait("tensor", s_wgu[slot], wl)
            wait_free("tensor", psB_free[slot])
            g_t = None
            for kk in range(KC):
                if g == 0:
                    wait("tensor", s_v, xn2_chunk_done[kk])
                for p in range(2):
                    g_t = cop("tensor", lambda e, sl=slot, k0=kk, pp=p: e.matmul(
                        psum[:, sl * 1024 + pp * 512:sl * 1024 + (pp + 1) * 512],
                        wgu[sl][:, k0, :], xn2[:, k0, pp * 512:(pp + 1) * 512],
                        start=(k0 == 0), stop=(k0 == KC - 1)),
                        s_t if (kk == KC - 1 and p == 1) else None)
            wgu_free[slot] = (s_t, g_t)
            wait("scalar", s_t, g_t)
            sv = cop("scalar", lambda e, gg=g, sl=slot: e.activation(
                out=h_sb[:, gg, :], in_=psum[:, sl * 1024:sl * 1024 + 1024],
                func=AF.Silu), s_sc)
            psB_free[slot] = (s_sc, sv)
            if tt == 0:
                pump.pump(7)
                if ag_done["u"] is None and pump.done_through(KC - 1):
                    trigger_ag("u", agi_u, ago_u)
                if ag_done["d"] is None and pump.done_through(KC + BB - 1):
                    trigger_ag("d", agi_dn, ago_dn)
        if tt == 0:
            if ag_done["u"] is None:
                pump.flush_matrix("u")
                trigger_ag("u", agi_u, ago_u)
            if ag_done["d"] is None:
                pump.flush_matrix("d")
                trigger_ag("d", agi_dn, ago_dn)

        wait("sync", s_cc, ag_done["u"])
        for g in range(OC):
            slot = g % 3
            r, b = g % 8, g // 8
            wait_free("sync", wgu_free[slot])
            wl = dma("sync", wgu[slot][:, :, :], ago_u[r, b], s_wgu[slot])
            wait("tensor", s_wgu[slot], wl)
            wait_free("tensor", psB_free[slot])
            g_t = None
            for kk in range(KC):
                for p in range(2):
                    g_t = cop("tensor", lambda e, sl=slot, k0=kk, pp=p: e.matmul(
                        psum[:, sl * 1024 + pp * 512:sl * 1024 + (pp + 1) * 512],
                        wgu[sl][:, k0, :], xn2[:, k0, pp * 512:(pp + 1) * 512],
                        start=(k0 == 0), stop=(k0 == KC - 1)),
                        s_t if (kk == KC - 1 and p == 1) else None)
            wgu_free[slot] = (s_t, g_t)
            wait("vector", s_t, g_t)
            hv = cop("vector", lambda e, gg=g, sl=slot: e.tensor_tensor(
                out=h_sb[:, gg, :], in0=h_sb[:, gg, :],
                in1=psum[:, sl * 1024:sl * 1024 + 1024], op=ALU.mult), s_v)
            psB_free[slot] = (s_v, hv)
        h_done_v = hv

        if DEBUG and tt == 0:
            wait("gpsimd", s_v, h_done_v)
            dma("gpsimd", dbg_h, h_sb[:, :, :], s_out)
            dma("gpsimd", dbg_wd, ago_dn, s_out)
            _hv = s_out.n
            wait("scalar", s_out, _hv)

        wait("sync", s_cc, ag_done["d"])
        for mm in range(KC):
            ws3 = mm % 2
            slot = mm % 2
            ps3 = mm % 3
            wait_free("sync", wdn_free[ws3])
            wl = None
            for r in range(8):
                wl = dma("sync", wdn[ws3][:, r * 6:(r + 1) * 6, :], ago_dn[r, mm], s_wdn[ws3])
            wait_free("scalar", xw_free[slot])
            xv = dma("scalar", xw[:, slot, :],
                     x1_d[mm * 128:(mm + 1) * 128, tt * 1024:(tt + 1) * 1024], s_xw[slot])
            wait("tensor", s_wdn[ws3], wl)
            if mm == 0:
                wait("tensor", s_v, h_done_v)
            wait_free("tensor", psB_free[ps3])
            d_t = None
            for g in range(OC):
                wslot = (g % 8) * 6 + g // 8
                for p in range(2):
                    d_t = cop("tensor", lambda e, w3=ws3, sl=ps3, gg=g, ws=wslot, pp=p: e.matmul(
                        psum[:, sl * 1024 + pp * 512:sl * 1024 + (pp + 1) * 512],
                        wdn[w3][:, ws, :], h_sb[:, gg, pp * 512:(pp + 1) * 512],
                        start=(gg == 0), stop=(gg == OC - 1)),
                        s_t if (g == OC - 1 and p == 1) else None)
            wdn_free[ws3] = (s_t, d_t)
            wait("vector", s_t, d_t)
            wait("vector", s_xw[slot], xv)
            wait_free("vector", ysb_free[0])
            yv = cop("vector", lambda e, sl=slot, p3=ps3: e.tensor_tensor(
                out=y_sb[:, 0, :], in0=psum[:, p3 * 1024:p3 * 1024 + 1024],
                in1=xw[:, sl, :], op=ALU.add), s_v)
            psB_free[ps3] = (s_v, yv)
            xw_free[slot] = (s_v, yv)
            wait("gpsimd", s_v, yv)
            ov = dma("gpsimd", y_d[mm * 128:(mm + 1) * 128, tt * 1024:(tt + 1) * 1024],
                     y_sb[:, 0, :], s_out)
            ysb_free[0] = (s_out, ov)
            if tt == 0:
                gi_f = gi.bitcast(F32)
                if mm == 0:
                    wait("scalar", s_v, gath_done_v[("d", BB - 1)])
                xn2_chunk_done[mm] = emit_xn2_chunk(
                    1, mm, gi_f[:, (mm % 2) * 1024:(mm % 2 + 1) * 1024],
                    mm % 2, xr2_free, sems=s_xa)
                if mm == KC - 1:
                    xn2_done[1] = xn2_chunk_done[mm]



    wait("sync", s_out, s_out.n)
    wait("gpsimd", s_out, s_out.n)

    with nc.Block() as block:
        @block.sync
        def _(e):
            for f in ops["sync"]:
                f(e)

        @block.gpsimd
        def _(e):
            for f in ops["gpsimd"]:
                f(e)

        @block.vector
        def _(e):
            for f in ops["vector"]:
                f(e)

        @block.tensor
        def _(e):
            for f in ops["tensor"]:
                f(e)

        @block.scalar
        def _(e):
            for f in ops["scalar"]:
                f(e)

    es.close()
    return nc


def _prep_inputs(L0_attn_x, Wq, bq, Wk, bk, Wv, bv, Wo, bo, lut, gidx, uidx, didx):
    x = np.asarray(L0_attn_x, np.float32)
    gidx = np.asarray(gidx, np.int32)
    uidx = np.asarray(uidx, np.int32)
    didx = np.asarray(didx, np.int32)
    in_maps = []
    for c in range(CORES):
        gblocks, ublocks, dblocks = [], [], []
        for b in range(BB):
            g = b * 8 + c
            if g >= OC:
                g = OC - 1
            gblocks.append(gidx[g * 128:(g + 1) * 128, :].T)
            ublocks.append(uidx[g * 128:(g + 1) * 128, :].T)
            dblocks.append(didx[:, g * 128:(g + 1) * 128].T)
        in_maps.append({
            "x": np.ascontiguousarray(x[c].T),
            "wq": np.ascontiguousarray(np.asarray(Wq, np.float32)),
            "wk": np.ascontiguousarray(np.asarray(Wk, np.float32)),
            "wv": np.ascontiguousarray(np.asarray(Wv, np.float32)),
            "wo": np.ascontiguousarray(np.asarray(Wo, np.float32)),
            "bq": np.ascontiguousarray(np.asarray(bq, np.float32).reshape(HD, 1)),
            "bk": np.ascontiguousarray(np.asarray(bk, np.float32).reshape(HD, 1)),
            "bv": np.ascontiguousarray(np.asarray(bv, np.float32).reshape(HD, 1)),
            "bo": np.ascontiguousarray(np.asarray(bo, np.float32).reshape(D, 1)),
            "lut": np.ascontiguousarray(np.asarray(lut, np.float32)),
            "gidx": np.ascontiguousarray(np.concatenate(gblocks, axis=1)),
            "uidx": np.ascontiguousarray(np.concatenate(ublocks, axis=1)),
            "didx": np.ascontiguousarray(np.concatenate(dblocks, axis=0)),
        })
    return in_maps


def kernel(L0_attn_x, L0_attn_Wq, L0_attn_bq, L0_attn_Wk, L0_attn_bk,
           L0_attn_Wv, L0_attn_bv, L0_attn_Wo, L0_attn_bo,
           L0_ffn_lut, L0_ffn_gate_idx, L0_ffn_up_idx, L0_ffn_down_idx):
    in_maps = _prep_inputs(L0_attn_x, L0_attn_Wq, L0_attn_bq, L0_attn_Wk,
                           L0_attn_bk, L0_attn_Wv, L0_attn_bv, L0_attn_Wo,
                           L0_attn_bo, L0_ffn_lut, L0_ffn_gate_idx,
                           L0_ffn_up_idx, L0_ffn_down_idx)
    nc = build()
    res = run_bass_kernel_spmd(nc, in_maps, core_ids=list(range(CORES)))
    out = np.stack([np.ascontiguousarray(r["y"].T) for r in res.results])
    return out.astype(np.float32)
